# version 08
# baseline (speedup 1.0000x reference)
import math
import numpy as np
import ml_dtypes
import concourse.bass as bass
import concourse.mybir as mybir
from concourse.bass_utils import run_bass_kernel_spmd
from concourse.bass_core_rust import IndirectOffsetOnAxis

F32 = mybir.dt.float32
BF16 = mybir.dt.bfloat16
U32 = mybir.dt.uint32
AF = mybir.ActivationFunctionType
ALU = mybir.AluOpType
AX = mybir.AxisListType

NCORE = 8
B, S, D = 4, 4096, 1024
TQ = 2048
DC = 8
H, HP, HD = 16, 8, 64
QB = 4
NE, CAP = 16, 1280
BT = B * S
EPS = 1e-5
SCALE = 1.0 / math.sqrt(HD)
BIG = 1.0e6


def _rope_cache_f32(seq_len, head_dim):
    import jax
    import jax.numpy as jnp
    with jax.default_device(jax.devices("cpu")[0]):
        pos = jnp.arange(seq_len, dtype=jnp.float32)
        freqs = 1.0 / 10000.0 ** (jnp.arange(0, head_dim, 2, dtype=jnp.float32) / head_dim)
        angles = pos[:, None] * freqs[None, :]
        cos = jnp.repeat(jnp.cos(angles), 2, axis=-1).astype(jnp.float32)
        sin = jnp.repeat(jnp.sin(angles), 2, axis=-1).astype(jnp.float32)
        return np.asarray(cos), np.asarray(sin)


def _host_constants():
    c = {}
    eye = np.eye(128, dtype=np.float32)
    c["identity_bf"] = eye.astype(ml_dtypes.bfloat16)
    c["identity_f32"] = eye
    R = np.zeros((128, 128), dtype=np.float32)
    for base in (0, 64):
        for j in range(32):
            R[base + 2 * j + 1, base + 2 * j] = -1.0
            R[base + 2 * j, base + 2 * j + 1] = 1.0
    c["rotP_bf"] = R.astype(ml_dtypes.bfloat16)
    c["ustrict_bf"] = np.triu(np.ones((128, 128), dtype=np.float32), 1) \
        .astype(ml_dtypes.bfloat16)
    kk = np.arange(128, dtype=np.float32)[:, None]
    qq = np.arange(512, dtype=np.float32)[None, :]
    c["DT_f32"] = np.ascontiguousarray(qq - kk)
    c["iotam_f32"] = np.ascontiguousarray(
        np.tile(np.arange(16, dtype=np.float32) - BIG, (128, 1)))
    grid = (np.arange(128, dtype=np.int64)[:, None] * 128
            + np.arange(128, dtype=np.int64)[None, :])
    c["agpos_f32"] = grid.astype(np.float32)
    c["iota128_f32"] = np.ascontiguousarray(
        np.tile(np.arange(128, dtype=np.float32), (128, 1)))
    c["iota20_f32"] = np.ascontiguousarray(
        np.tile(np.arange(20, dtype=np.float32), (128, 1)))
    c["iotap_f32"] = np.ascontiguousarray(
        np.arange(128, dtype=np.float32).reshape(128, 1))
    base = (np.arange(16) // 2) * (2 * CAP) + (np.arange(16) % 2) * CAP
    c["baseB_f32"] = np.ascontiguousarray(
        np.tile(base.astype(np.float32), (128, 1)))
    c["onesrow_bf"] = np.ones((1, 128), np.float32).astype(ml_dtypes.bfloat16)
    c["onescol_bf"] = np.ones((128, 1), np.float32).astype(ml_dtypes.bfloat16)
    return c


def _per_core_inputs(inputs, core):
    b, half = core // 2, core % 2
    x = np.asarray(inputs["L0_attn_x"])
    xb = np.ascontiguousarray(x[b].T)
    cols = np.concatenate([np.arange(512 * (2 * j + half),
                                     512 * (2 * j + half) + 512)
                           for j in range(QB)])
    d = {}
    d["xkv"] = xb
    d["xq"] = np.ascontiguousarray(xb[:, cols])
    d["wqT"] = np.ascontiguousarray(np.asarray(inputs["L0_attn_wq"]).T)
    d["wkT"] = np.ascontiguousarray(np.asarray(inputs["L0_attn_wk"]).T)
    d["wvT"] = np.ascontiguousarray(np.asarray(inputs["L0_attn_wv"]).T)
    d["woT"] = np.ascontiguousarray(np.asarray(inputs["L0_attn_wo"]).T)
    d["wmod"] = np.ascontiguousarray(np.asarray(inputs["L0_wmod"]))
    d["cfeat"] = np.ascontiguousarray(
        np.asarray(inputs["L0_c"])[b].reshape(8, 128).T)
    d["wg"] = np.ascontiguousarray(np.asarray(inputs["L0_ffn_Wg"]))
    ea, eb = 2 * core, 2 * core + 1
    W1 = np.asarray(inputs["L0_ffn_W1"])
    W2 = np.asarray(inputs["L0_ffn_W2"])
    b1 = np.asarray(inputs["L0_ffn_b1"])
    b2 = np.asarray(inputs["L0_ffn_b2"])
    d["w1a"], d["w1b"] = np.ascontiguousarray(W1[ea]), np.ascontiguousarray(W1[eb])
    d["w2a"], d["w2b"] = np.ascontiguousarray(W2[ea]), np.ascontiguousarray(W2[eb])
    d["b1p"] = np.ascontiguousarray(np.concatenate(
        [b1[e].reshape(16, 128).T for e in (ea, eb)], axis=1))
    d["b2pa"] = np.ascontiguousarray(b2[ea:ea + 1])
    d["b2pb"] = np.ascontiguousarray(b2[eb:eb + 1])
    cos, sin = _rope_cache_f32(S, HD)
    d["cosk"] = np.ascontiguousarray(np.tile(cos.T, (2, 1)))
    d["sink"] = np.ascontiguousarray(np.tile(sin.T, (2, 1)))
    d["cosq"] = np.ascontiguousarray(np.tile(cos[cols].T, (2, 1)))
    d["sinq"] = np.ascontiguousarray(np.tile(sin[cols].T, (2, 1)))
    d["thetas"] = np.ascontiguousarray(np.tile(
        (128.0 * np.arange(8) - 512.0 * half).astype(np.float32), (128, 1)))
    lb = np.full(16, BIG, np.float32)
    lb[ea], lb[eb] = 0.0, float(CAP)
    d["localbase"] = np.ascontiguousarray(np.tile(lb, (128, 1)))
    P = np.zeros((128, 128), dtype=np.float32)
    mine = list(range(16 * core, 16 * core + 16))
    order = mine + [r for r in range(128) if r not in mine]
    for newcol, row in enumerate(order):
        P[row, newcol] = 1.0
    d["permT"] = P
    return d


class Emitter:
    ENGS = ("pe", "act", "dve", "pool", "sp")

    def __init__(self, nc):
        self.nc = nc
        self.q = {e: [] for e in self.ENGS}
        self.cnt = {}
        self.mute = False

    def val(self, sem):
        return self.cnt.get(sem.name, 0)

    def op(self, eng, fn, *args, inc=None, amt=None, lp=False, **kw):
        if self.mute:
            return 0
        if inc is not None:
            amt = amt if amt is not None else (16 if "dma" in fn else 1)
            self.cnt[inc.name] = self.cnt.get(inc.name, 0) + amt

        def run(e, fn=fn, args=args, kw=kw, inc=inc, amt=amt, lp=lp):
            if lp:
                with self.nc.allow_low_precision("bf16 rounding within tolerance"):
                    r = getattr(e, fn)(*args, **kw)
            else:
                r = getattr(e, fn)(*args, **kw)
            if inc is not None:
                r.then_inc(inc, amt)
        self.q[eng].append(run)
        return self.cnt.get(inc.name, 0) if inc is not None else None

    def wait(self, eng, sem, val=None):
        if self.mute:
            return
        v = self.val(sem) if val is None else val
        self.q[eng].append(lambda e, s=sem, v=v: e.wait_ge(s, v))

    def inc(self, eng, sem, amt=1):
        if self.mute:
            return 0
        self.cnt[sem.name] = self.cnt.get(sem.name, 0) + amt
        v = self.cnt[sem.name]
        self.q[eng].append(lambda e, s=sem, a=amt: e.sem_inc(s, a))
        return v

    def realize(self, block):
        q = self.q

        @block.tensor
        def _(e):
            for f in q["pe"]:
                f(e)

        @block.scalar
        def _(e):
            for f in q["act"]:
                f(e)

        @block.vector
        def _(e):
            for f in q["dve"]:
                f(e)

        @block.gpsimd
        def _(e):
            for f in q["pool"]:
                f(e)

        @block.sync
        def _(e):
            for f in q["sp"]:
                f(e)


def build_program(taps=()):
    import os
    LV = int(os.environ.get("KLV", "99"))
    KS = int(os.environ.get("KSUB", "99"))
    NOAG = os.environ.get("KNOAG", "0") == "1"
    nc = bass.Bass("TRN2", target_bir_lowering=False,
                   detect_race_conditions=False)

    din = {}
    for name, shape in [
        ("xkv", [D, S]), ("xq", [D, TQ]),
        ("wqT", [D, D]), ("wkT", [D, D]), ("wvT", [D, D]), ("woT", [D, D]),
        ("wmod", [D, 6 * D]), ("cfeat", [128, 8]), ("wg", [D, 16]),
        ("w1a", [D, 2048]), ("w1b", [D, 2048]),
        ("w2a", [2048, D]), ("w2b", [2048, D]),
        ("b1p", [128, 32]), ("b2pa", [1, D]), ("b2pb", [1, D]),
        ("cosk", [128, S]), ("sink", [128, S]),
        ("cosq", [128, TQ]), ("sinq", [128, TQ]),
        ("thetas", [128, 8]), ("localbase", [128, 16]),
        ("DT_f32", [128, 512]), ("iotam_f32", [128, 16]),
        ("agpos_f32", [128, 128]), ("baseB_f32", [128, 16]),
        ("permT", [128, 128]), ("identity_f32", [128, 128]),
    ]:
        din[name] = nc.dram_tensor(name, shape, F32, kind="ExternalInput").ap()
    for name in ("identity_bf", "rotP_bf", "ustrict_bf"):
        din[name] = nc.dram_tensor(name, [128, 128], BF16,
                                   kind="ExternalInput").ap()
    din["onesrow_bf"] = nc.dram_tensor("onesrow_bf", [1, 128], BF16,
                                       kind="ExternalInput").ap()
    din["onescol_bf"] = nc.dram_tensor("onescol_bf", [128, 1], BF16,
                                       kind="ExternalInput").ap()
    for name, shape in [("iota128_f32", [128, 128]), ("iota20_f32", [128, 20]),
                        ("iotap_f32", [128, 1])]:
        din[name] = nc.dram_tensor(name, shape, F32, kind="ExternalInput").ap()

    yout = nc.dram_tensor("yout", [D, TQ], F32, kind="ExternalOutput").ap()
    tap_out = {}
    for tname, tshape, tdt in taps:
        tap_out[tname] = nc.dram_tensor("tap_" + tname, tshape, tdt,
                                        kind="ExternalOutput").ap()

    rt_in = nc.dram_tensor("rt_in", [TQ, 2], F32)
    rt_out = nc.dram_tensor("rt_out", [BT, 2], F32)
    h2d = nc.dram_tensor("h2d", [TQ, D], BF16)
    h2ag = nc.dram_tensor("h2ag", [BT, D], BF16)
    x1d = nc.dram_tensor("x1d", [D, TQ], BF16)
    modsc = nc.dram_tensor("modsc", [1, 1024], F32)
    outblk = nc.dram_tensor("outblk", [2 * CAP, D], BF16)
    outag = nc.dram_tensor("outag", [NCORE * 2 * CAP, D], BF16)

    SBUF_BASE = 17408
    SBUF_LIMIT = 229344
    cursor = [SBUF_BASE]
    offs = {}

    def _bytes(shape, dt):
        per = {F32: 4, BF16: 2, U32: 4}[dt]
        free = 1
        for s_ in shape[1:]:
            free *= s_
        return free * per

    def alloc(name, shape, dt, at=None):
        if at is None:
            off = (cursor[0] + 31) // 32 * 32
            cursor[0] = off + _bytes(shape, dt)
        else:
            off = at
        assert off + _bytes(shape, dt) <= SBUF_LIMIT, (name, off)
        offs[name] = off
        return nc.alloc_sbuf_tensor_at(name, shape, dt, offset=off)

    c_id_bf = alloc("c_id_bf", [128, 128], BF16)
    c_rot = alloc("c_rot", [128, 128], BF16)
    c_ones_r = alloc("c_ones_r", [1, 128], BF16)
    c_ones_c = alloc("c_ones_c", [128, 1], BF16)
    c_theta = alloc("c_theta", [128, 8], F32)
    c_iotam = alloc("c_iotam", [128, 16], F32)
    m_c = alloc("m_c", [128, 8], F32)
    m_silu = alloc("m_silu", [128, 8], BF16)
    m_ga = alloc("m_ga", [128, 8], F32)
    m_gb = alloc("m_gb", [128, 8], F32)
    m_sc1 = alloc("m_sc1", [128, 8], F32)
    m_sc2 = alloc("m_sc2", [128, 8], F32)
    m_t8 = alloc("m_t8", [8, 128], F32)
    m_r1 = alloc("m_r1", [1, 1024], BF16)
    m_rowE = alloc("m_rowE", [2, 1024], BF16)
    m_rowE2 = alloc("m_rowE2", [2, 1024], BF16)
    w_g = alloc("w_g", [128, DC * 16], BF16)
    b1_sb = alloc("b1_sb", [128, 32], F32)
    r_rows = alloc("r_rows", [1, 1536], F32)
    r_bf = alloc("r_bf", [128, 512], BF16)
    s_recip_t = alloc("s_recip_t", [1, 512], BF16)
    r_F = alloc("r_F", [2, 512], BF16)
    xsq = alloc("xsq", [128, 512], BF16)
    kraw = alloc("kraw", [128, 512], BF16)
    kt1 = alloc("kt1", [128, 512], BF16)
    kt2 = alloc("kt2", [128, 512], BF16)
    s_bcast = alloc("s_bcast", [64, 512], BF16)
    lgmax = alloc("lgmax", [128, 1], F32)
    expg = alloc("expg", [128, 16], F32)
    sumg = alloc("sumg", [128, 1], F32)
    valg = alloc("valg", [128, 1], F32)
    vig = alloc("vig", [128, 16], F32)
    eg = alloc("eg", [128, 1], F32)
    rec2 = alloc("rec2", [128, 2], F32)
    idxg = alloc("idxg", [128, 1], U32)
    zgb = alloc("zgb", [128, 128], BF16)
    g_feU = alloc("g_feU", [128, 128], U32)
    g_selT = alloc("g_selT", [128, 128], F32)
    g_gU = alloc("g_gU", [128, 20], U32)

    x_sb = alloc("x_sb", [128, DC * 512], BF16)
    h1_sb = alloc("h1_sb", [128, DC * 512], BF16)
    qT = alloc("qT", [128, DC * 512], BF16)
    mrg = alloc("mrg", [128, HP * 512], BF16)
    expT = alloc("expT", [128, 2048], BF16)
    wvq = alloc("wvq", [128, DC * 256], BF16)
    ropeA = alloc("ropeA", [128, 512], BF16)
    ropeB = alloc("ropeB", [128, 512], BF16)
    wkbuf = alloc("wkbuf", [128, 2 * 1024], BF16)
    mask8 = alloc("mask8", [128, 8 * 512], BF16)
    kT = alloc("kT", [128, DC * S], BF16)
    v_sb = alloc("v_sb", [128, 32 * 1040], BF16)

    x0q = h1_sb
    x1_sb = qT
    h2_sb = mrg
    h2t_v = expT
    big1, big2 = offs["kT"], offs["v_sb"]
    c_dt = alloc("c_dt", [128, 512], F32, at=offs["wkbuf"])
    m_wch = alloc("m_wch", [128, 2 * 512], BF16, at=offs["wvq"])
    m_row = alloc("m_row", [1, 6 * D], F32, at=big2)
    c_idf = alloc("c_idf", [128, 128], F32, at=big2 + 24576)
    g_oh = alloc("g_oh", [128, 2048], F32, at=big2)
    g_pfA = alloc("g_pfA", [128, 2048], F32, at=big2 + 8192)
    g_pfB = alloc("g_pfB", [128, 2048], F32, at=big2 + 16384)
    g_rank = alloc("g_rank", [128, 2048], F32, at=big2 + 24576)
    g_pk = alloc("g_pk", [128, 256], F32, at=big2 + 32768)
    g_val = alloc("g_val", [128, 128], F32, at=big2 + 33792)
    g_e = alloc("g_e", [128, 128], F32, at=big2 + 34304)
    g_T = alloc("g_T", [128, 16], BF16, at=big2 + 34816)
    g_rt = alloc("g_rt", [128, 128], F32, at=big2 + 34848)
    g_fetch = alloc("g_fetch", [128, 128], F32, at=big2 + 35360)
    g_fb = alloc("g_fb", [128, 128], F32, at=big2 + 35872)
    g_scat = alloc("g_scat", [128, 128], F32, at=big2 + 36384)
    g_tmp = alloc("g_tmp", [128, 128], F32, at=big2 + 36896)
    g_sel = alloc("g_sel", [128, 128], F32, at=big2 + 37408)
    c_us = alloc("c_us", [128, 128], BF16, at=big2 + 37920)
    c_agpos = alloc("c_agpos", [128, 128], F32, at=big2 + 38176)
    c_base = alloc("c_base", [128, 16], F32, at=big2 + 38688)
    c_lbase = alloc("c_lbase", [128, 16], F32, at=big2 + 38752)
    c_perm = alloc("c_perm", [128, 128], F32, at=big2 + 38816)
    c_i128 = alloc("c_i128", [128, 128], F32, at=big2 + 39328 + 192)
    c_i20 = alloc("c_i20", [128, 20], F32, at=big2 + 39328 + 192 + 512)
    c_iotap = alloc("c_iotap", [128, 1], F32, at=big2 + 39328)
    g_sT = alloc("g_sT", [128, 128], F32, at=big2 + 40128)
    g_lhs = alloc("g_lhs", [128, 128], F32, at=big2 + 40640)
    g_rhs = alloc("g_rhs", [128, 20], F32, at=big2 + 41152)
    g_tsc = alloc("g_tsc", [128, 1], F32, at=big2 + 41248)
    g_lhs2 = alloc("g_lhs2", [128, 128], F32, at=big2 + 24576)
    g_rhs2 = alloc("g_rhs2", [128, 20], F32, at=big2 + 25088)
    g_selB = alloc("g_selB", [128, 128], BF16, at=big2 + 25184)
    w1_sb = alloc("w1_sb", [128, 2 * DC * 2048], BF16, at=big1)
    xe_sb = alloc("xe_sb", [128, 10 * 1024], BF16, at=big2 + 41312)
    hTs = alloc("hTs", [128, 16 * 512], BF16, at=big2 + 8192)
    w2_sb = alloc("w2_sb", [128, 16 * 1024], BF16, at=offs["x_sb"])
    xeT = alloc("xeT", [128, DC * 512], BF16, at=offs["expT"])
    b2B = alloc("b2B", [128, 1024], BF16, at=offs["ropeA"])
    b2rowa = alloc("b2rowa", [1, 1024], BF16, at=big2 + 62048)
    b2rowb = alloc("b2rowb", [1, 1024], BF16, at=big2 + 64096)
    yf_sb = alloc("yf_sb", [128, 1024], F32, at=offs["wkbuf"])
    z_sb = alloc("z_sb", [128, 1024], BF16, at=offs["mask8"])
    x1r = alloc("x1r", [128, 1024], BF16, at=offs["mask8"] + 2048)
    yo_sb = alloc("yo_sb", [128, 1024], BF16, at=offs["mask8"] + 4096)
    h2row = alloc("h2row", [128, 1024], BF16, at=offs["mask8"] + 4096)
    zsel = alloc("zsel", [128, 1024], BF16, at=offs["mask8"] + 6144)
    zdiff = alloc("zdiff", [128, 1024], BF16, at=offs["ropeA"])

    assert cursor[0] <= SBUF_LIMIT, cursor[0]
    assert offs["v_sb"] + 66560 <= SBUF_LIMIT, offs["v_sb"]

    r_mean = r_rows[0:1, 0:512]
    r_ex2 = r_rows[0:1, 512:1024]
    r_var = r_rows[0:1, 1024:1536]
    r_rstd = r_bf[0:1, :]
    s_recip = s_recip_t[0:1, :]

    from contextlib import ExitStack
    es = ExitStack()
    psa = lambda n_, s_, d_=F32: es.enter_context(nc.psum_tensor(n_, s_, d_))
    sem = lambda n_: es.enter_context(nc.semaphore(n_))

    with es:
        ps_A = psa("ps_A", [128, 2048])
        ps_B = psa("ps_B", [128, 512])
        ps_o65a = psa("ps_o65a", [65, 512])
        ps_o65b = psa("ps_o65b", [65, 512])
        ps_tp = psa("ps_tp", [128, 128], BF16)
        A_E = ps_A[:, 0:512]
        A_F = ps_A[:, 512:1024]
        A_s1 = ps_A[0:1, 1024:1536]
        A_s2 = ps_A[0:1, 1536:2048]
        A_rot = ps_A[:, 1024:1536]
        A_bc = ps_A[0:64, 0:512]
        A_mod = ps_A[0:1, 0:512]
        A_offs = ps_A[:, 0:16]
        A_tpf = ps_A[:, 0:128]
        B_lg = ps_B[:, 0:16]

        s_cld = sem("s_cld")
        s_cldg = sem("s_cldg")
        s_wmod = sem("s_wmod")
        s_modb = sem("s_modb")
        s_wkb = sem("s_wkb")
        s_whalf = sem("s_whalf")
        s_rope = sem("s_rope")
        s_xin = sem("s_xin")
        s_xfree = sem("s_xfree")
        s_x0 = sem("s_x0")
        s_p3c = sem("s_p3c")
        s_pe = sem("s_pe")
        s_act = sem("s_act")
        s_dve = sem("s_dve")
        s_sp = sem("s_sp")
        s_cc = sem("s_cc")
        s_wf = sem("s_wf")
        s_gx = sem("s_gx")
        s_out = sem("s_out")
        s_fin = sem("s_fin")
        s_tap = sem("s_tap")

        em = Emitter(nc)
        E = em.op
        W = em.wait

        def tap(name, src_ap):
            if name in tap_out:
                W("sp", s_dve)
                W("sp", s_act)
                W("sp", s_pe)
                E("sp", "dma_start", inc=s_tap, out=tap_out[name], in_=src_ap)

        for dst, src in [
            (c_theta, din["thetas"]), (c_iotam, din["iotam_f32"]),
            (c_id_bf, din["identity_bf"]), (c_rot, din["rotP_bf"]),
            (c_ones_r, din["onesrow_bf"]), (c_ones_c, din["onescol_bf"]),
            (m_c, din["cfeat"]), (b1_sb, din["b1p"]),
            (c_dt, din["DT_f32"]), (c_idf, din["identity_f32"]),
        ]:
            E("sp", "dma_start", inc=s_cld, out=dst[:, :], in_=src)
        E("pool", "dma_start", inc=s_cldg,
          out=w_g[:, :].rearrange("p (dc e) -> p dc e", dc=DC),
          in_=din["wg"].rearrange("(dc p) e -> p dc e", p=128))

        W("dve", s_cld)
        for w in range(8):
            E("dve", "tensor_scalar", mask8[:, 512 * w:512 * w + 512],
              c_dt[:, :], c_theta[:, w:w + 1], None, ALU.is_ge, inc=s_dve)
        E("dve", "memset", r_F[:, :], 1.0, inc=s_dve)

        W("act", s_cld)
        E("act", "activation", inc=s_act, out=m_silu[:, :], in_=m_c[:, :],
          func=AF.Silu)
        act_silu = em.val(s_act)
        NMOD = 12
        for part in range(NMOD):
            for dc in range(DC):
                it = part * DC + dc
                if it >= 2:
                    W("pool", s_pe, it - 1)
                E("pool", "dma_start", inc=s_wmod,
                  out=m_wch[:, 512 * (it % 2):512 * (it % 2) + 512],
                  in_=din["wmod"][128 * dc:128 * dc + 128,
                                  512 * part:512 * part + 512])
        W("pe", s_act, act_silu)
        W("pe", s_cld)
        for part in range(NMOD):
            for dc in range(DC):
                it = part * DC + dc
                W("pe", s_wmod, 16 * (it + 1))
                E("pe", "matmul", A_mod, m_silu[:, dc:dc + 1],
                  m_wch[:, 512 * (it % 2):512 * (it % 2) + 512],
                  start=(dc == 0), stop=(dc == DC - 1), inc=s_pe)
            W("act", s_pe)
            E("act", "activation", inc=s_act,
              out=m_row[:, 512 * part:512 * part + 512], in_=A_mod,
              func=AF.Identity)
        act_modrow = em.val(s_act)

        W("dve", s_act, act_modrow)
        E("dve", "tensor_scalar", m_rowE[0:1, :], m_row[:, 1024:2048], 1.0,
          None, ALU.add, inc=s_dve)
        E("dve", "tensor_scalar", m_rowE2[0:1, :], m_row[:, 4096:5120], 1.0,
          None, ALU.add, inc=s_dve)
        E("dve", "tensor_scalar", m_r1[:, :], m_row[:, 0:1024], -1.0,
          None, ALU.mult, inc=s_dve)
        W("sp", s_dve)
        E("sp", "dma_start", inc=s_modb, out=m_rowE[1:2, :], in_=m_r1[:, :])
        W("dve", s_modb, em.val(s_modb))
        E("dve", "tensor_scalar", m_r1[:, :], m_row[:, 3072:4096], -1.0,
          None, ALU.mult, inc=s_dve)
        W("sp", s_dve)
        E("sp", "dma_start", inc=s_modb, out=m_rowE2[1:2, :], in_=m_r1[:, :])

        for off, dstcol in ((2048, m_ga), (5120, m_gb)):
            W("sp", s_act, act_modrow)
            W("sp", s_pe)
            E("sp", "dma_start", inc=s_modb, out=modsc.ap(),
              in_=m_row[0:1, off:off + 1024])
            W("sp", s_modb)
            E("sp", "dma_start", inc=s_modb, out=m_t8[:, :],
              in_=modsc.ap().rearrange("one (a bq) -> a (one bq)", a=8))
            W("pe", s_modb)
            E("pe", "transpose", A_tpf[:, 0:8], m_t8[:, :], c_idf[0:8, 0:8],
              inc=s_pe)
            W("act", s_pe)
            W("act", s_dve)
            E("act", "activation", inc=s_act, out=m_sc1[:, :],
              in_=A_tpf[:, 0:8], func=AF.Sign)
            E("act", "activation", inc=s_act, out=m_sc2[:, :],
              in_=A_tpf[:, 0:8], func=AF.Abs)
            W("dve", s_act)
            E("dve", "tensor_scalar", m_sc2[:, :], m_sc2[:, :], 0.25, None,
              ALU.max, inc=s_dve)
            W("dve", s_dve)
            E("dve", "tensor_mul", dstcol[:, :], m_sc1[:, :], m_sc2[:, :],
              inc=s_dve)
        act_mod_done = em.val(s_act)
        dve_mod_done = em.val(s_dve)

        W("dve", s_modb, em.val(s_modb))
        E("dve", "memset", v_sb[:, :], 1.0, inc=s_dve)
        dve_vones = em.val(s_dve)

        em.mute = LV < 1
        xkv3 = din["xkv"].rearrange("(dc p) t -> p dc t", p=128)
        xq3 = din["xq"].rearrange("(dc p) t -> p dc t", p=128)
        wk4 = din["wkT"].rearrange("(dc p) (oc o) -> p oc dc o", p=128, o=128)
        wv4 = din["wvT"].rearrange("(dc p) (dvq o) -> p dvq dc o", p=128, o=256)
        wq4 = din["wqT"].rearrange("(dc p) (oc o) -> p oc dc o", p=128, o=128)
        wo4 = din["woT"].rearrange("(dc p) (oc o) -> p oc dc o", p=128, o=128)

        def ln_block(src, srcoff, dst, rows):
            for dc in range(DC):
                o = srcoff + dc * 512
                W("dve", s_pe)
                E("dve", "tensor_mul", xsq[:, :], src[:, o:o + 512],
                  src[:, o:o + 512], inc=s_dve)
                W("pe", s_dve)
                E("pe", "matmul", A_s1, c_ones_c[:, 0:1], src[:, o:o + 512],
                  start=(dc == 0), stop=(dc == DC - 1))
                E("pe", "matmul", A_s2, c_ones_c[:, 0:1], xsq[:, :],
                  start=(dc == 0), stop=(dc == DC - 1), inc=s_pe)
            W("act", s_pe)
            W("act", s_dve)
            E("act", "activation", inc=s_act, out=r_mean, in_=A_s1,
              func=AF.Identity, scale=1.0 / D)
            E("act", "activation", inc=s_act, out=r_ex2, in_=A_s2,
              func=AF.Identity, scale=1.0 / D)
            W("dve", s_act)
            E("dve", "tensor_mul", r_var, r_mean, r_mean, inc=s_dve)
            W("dve", s_dve)
            E("dve", "tensor_sub", r_var, r_ex2, r_var, inc=s_dve)
            W("dve", s_dve)
            E("dve", "tensor_scalar", r_var, r_var, EPS, None, ALU.add,
              inc=s_dve)
            W("act", s_dve)
            E("act", "activation", inc=s_act, out=r_var, in_=r_var,
              func=AF.Ln)
            W("act", s_act)
            E("act", "activation", inc=s_act, out=r_rstd, in_=r_var,
              func=AF.Exp, scale=-0.5)
            W("dve", s_act)
            E("dve", "tensor_mul", r_F[0:1, :], r_mean, r_rstd, inc=s_dve)
            for dc in range(DC):
                o = srcoff + dc * 512
                W("pe", s_dve)
                W("pe", s_act)
                E("pe", "matmul", A_E, rows[0:1, 128 * dc:128 * dc + 128],
                  r_rstd, start=True, stop=True)
                E("pe", "matmul", A_F, rows[:, 128 * dc:128 * dc + 128],
                  r_F[:, :], start=True, stop=True, inc=s_pe)
                W("dve", s_pe)
                E("dve", "tensor_mul", dst[:, dc * 512:dc * 512 + 512],
                  src[:, o:o + 512], A_E, inc=s_dve)
                W("dve", s_dve)
                E("dve", "tensor_sub", dst[:, dc * 512:dc * 512 + 512],
                  dst[:, dc * 512:dc * 512 + 512], A_F, inc=s_dve)

        def stream_rope(cos_src, sin_src, coloff):
            W("pool", s_dve)
            E("pool", "dma_start", inc=s_rope, out=ropeA[:, :],
              in_=cos_src[:, coloff:coloff + 512])
            E("pool", "dma_start", inc=s_rope, out=ropeB[:, :],
              in_=sin_src[:, coloff:coloff + 512])

        def proj_stream(w4, dst, dstoff, dststride):
            base = em.val(s_wkb)
            for oc in (0, 1):
                W("pool", s_pe)
                W("pool", s_dve)
                E("pool", "dma_start", inc=s_wkb,
                  out=wkbuf[:, (oc % 2) * 1024:(oc % 2) * 1024 + 1024]
                  .rearrange("p (dc o) -> p dc o", dc=DC), in_=w4[:, oc])
            W("dve", s_rope, em.val(s_rope))
            for oc in range(DC):
                W("pe", s_wkb, base + 16 * (oc + 1))
                for dc in range(DC):
                    E("pe", "matmul", ps_B[:, :],
                      wkbuf[:, (oc % 2) * 1024 + dc * 128:(oc % 2) * 1024 + dc * 128 + 128],
                      h1_sb[:, dc * 512:dc * 512 + 512],
                      start=(dc == 0), stop=(dc == DC - 1),
                      inc=s_pe if dc == DC - 1 else None)
                if oc + 2 < DC:
                    W("pool", s_pe)
                    E("pool", "dma_start", inc=s_wkb,
                      out=wkbuf[:, (oc % 2) * 1024:(oc % 2) * 1024 + 1024]
                      .rearrange("p (dc o) -> p dc o", dc=DC),
                      in_=w4[:, oc + 2])
                W("act", s_pe)
                W("act", s_dve)
                E("act", "activation", inc=s_act, out=kraw[:, :], in_=ps_B[:, :],
                  func=AF.Identity)
                W("pe", s_act)
                E("pe", "matmul", A_rot, c_rot[:, :], kraw[:, :], start=True,
                  stop=True, inc=s_pe)
                W("dve", s_act)
                E("dve", "tensor_mul", kt1[:, :], kraw[:, :], ropeA[:, :],
                  inc=s_dve)
                W("dve", s_pe)
                E("dve", "tensor_mul", kt2[:, :], A_rot, ropeB[:, :],
                  inc=s_dve)
                W("dve", s_dve)
                E("dve", "tensor_add",
                  dst[:, oc * dststride + dstoff:oc * dststride + dstoff + 512],
                  kt1[:, :], kt2[:, :], inc=s_dve)

        for tb in range(8):
            if tb >= 1:
                W("pool", s_xfree, tb)
            E("pool", "dma_start", inc=s_xin,
              out=x_sb[:, :].rearrange("p (dc t) -> p dc t", dc=DC),
              in_=xkv3[:, :, 512 * tb:512 * tb + 512])
            W("dve", s_xin, 16 * (tb + 1))
            if tb == 0:
                W("dve", s_dve, dve_vones)
                W("dve", s_act, act_mod_done)
            ln_block(x_sb, 0, h1_sb, m_rowE)
            stream_rope(din["cosk"], din["sink"], 512 * tb)
            proj_stream(wk4, kT, 512 * tb, S)
            vcopy_done = {}
            for dvq in range(4):
                W("pool", s_pe)
                E("pool", "dma_start", inc=s_whalf,
                  out=wvq[:, :].rearrange("p (dc o) -> p dc o", dc=DC),
                  in_=wv4[:, dvq])
                W("pe", s_whalf, em.val(s_whalf))
                for tt in range(4):
                    ti2 = dvq * 4 + tt
                    vps = ps_B[:, 0:256] if tt % 2 == 0 else ps_A[:, 0:256]
                    if ti2 >= 2:
                        W("pe", s_dve, vcopy_done[ti2 - 2])
                    for dc in range(DC):
                        E("pe", "matmul", vps,
                          h1_sb[:, dc * 512 + tt * 128:dc * 512 + tt * 128 + 128],
                          wvq[:, dc * 256:dc * 256 + 256],
                          start=(dc == 0), stop=(dc == DC - 1),
                          inc=s_pe if dc == DC - 1 else None)
                    ti = tb * 4 + tt
                    W("dve", s_pe)
                    dstv = v_sb[:, ti * 1040 + dvq * 260:ti * 1040 + dvq * 260 + 260] \
                        .rearrange("p (h c) -> p h c", h=4)[:, :, 0:64]
                    srcv = vps.rearrange("p (h c) -> p h c", h=4)
                    vcopy_done[ti2] = E("dve", "tensor_copy", dstv, srcv, inc=s_dve)
                W("pe", s_dve)
            W("pool", s_dve)
            em.inc("pool", s_xfree)

        em.mute = LV < 2
        for qb in range(QB):
            W("pool", s_xfree, 8 + qb)
            E("pool", "dma_start", inc=s_xin,
              out=x_sb[:, :].rearrange("p (dc t) -> p dc t", dc=DC),
              in_=xq3[:, :, 512 * qb:512 * qb + 512])
            W("dve", s_xin, 16 * (9 + qb))
            ln_block(x_sb, 0, h1_sb, m_rowE)
            stream_rope(din["cosq"], din["sinq"], 512 * qb)
            proj_stream(wq4, qT, 0, 512)
            W("pool", s_pe)
            E("pool", "dma_start", inc=s_x0,
              out=x0q[:, :].rearrange("p (dc t) -> p dc t", dc=DC),
              in_=xq3[:, :, 512 * qb:512 * qb + 512])

            ktiles = 8 * qb + 8
            for hp in range(HP):
                if hp == 0:
                    W("act", s_sp)
                first_av = True
                for pr in range(ktiles // 2):
                    W("pe", s_act)
                    W("pe", s_dve)
                    for r in range(2):
                        kt = 2 * pr + r
                        for h01 in range(2):
                            reg = (r * 2 + h01) * 512
                            E("pe", "matmul", ps_A[:, reg:reg + 512],
                              kT[64 * h01:64 * h01 + 64,
                                 hp * S + kt * 128:hp * S + kt * 128 + 128],
                              qT[64 * h01:64 * h01 + 64, hp * 512:hp * 512 + 512],
                              start=True, stop=True,
                              tile_position=(64 * h01, 0),
                              inc=s_pe if (r == 1 and h01 == 1) else None)
                    W("act", s_pe)
                    W("act", s_dve)
                    E("act", "activation", inc=s_act, out=expT[:, :],
                      in_=ps_A[:, :], func=AF.Exp, scale=SCALE)
                    w0 = 2 * pr - (ktiles - 8)
                    if w0 >= 0:
                        W("dve", s_act)
                        for r in range(2):
                            mk = mask8[:, (w0 + r) * 512:(w0 + r) * 512 + 512] \
                                .rearrange("p (one t) -> p one t", one=1) \
                                .broadcast_to((128, 2, 512))
                            E("dve", "tensor_mul",
                              expT[:, r * 1024:r * 1024 + 1024]
                              .rearrange("p (h t) -> p h t", h=2),
                              expT[:, r * 1024:r * 1024 + 1024]
                              .rearrange("p (h t) -> p h t", h=2), mk,
                              inc=s_dve)
                        W("pe", s_dve)
                    else:
                        W("pe", s_act)
                    for r in range(2):
                        kt = 2 * pr + r
                        for h01 in range(2):
                            dst65 = ps_o65a if h01 == 0 else ps_o65b
                            E("pe", "matmul", dst65[:, :],
                              v_sb[:, kt * 1040 + (2 * hp + h01) * 65:
                                   kt * 1040 + (2 * hp + h01) * 65 + 65],
                              expT[:, (r * 2 + h01) * 512:(r * 2 + h01) * 512 + 512],
                              start=(first_av and r == 0),
                              stop=(pr == ktiles // 2 - 1 and r == 1),
                              inc=s_pe if (r == 1 and h01 == 1) else None)
                    first_av = False
                    W("act", s_pe)
                for h01 in range(2):
                    src65 = ps_o65a if h01 == 0 else ps_o65b
                    W("dve", s_pe)
                    E("dve", "reciprocal", s_recip, src65[64:65, :], inc=s_dve, lp=True)
                    W("pe", s_dve)
                    E("pe", "matmul", A_bc, c_ones_r[0:1, 0:64], s_recip,
                      start=True, stop=True, inc=s_pe)
                    W("act", s_pe)
                    W("act", s_dve)
                    E("act", "activation", inc=s_act, out=s_bcast[:, :],
                      in_=A_bc, func=AF.Identity)
                    W("dve", s_act)
                    E("dve", "tensor_mul",
                      mrg[64 * h01:64 * h01 + 64, hp * 512:hp * 512 + 512],
                      src65[0:64, :], s_bcast[:, :], inc=s_dve)

            base_o = em.val(s_wkb)
            for oc in (0, 1):
                W("pool", s_pe)
                E("pool", "dma_start", inc=s_wkb,
                  out=wkbuf[:, (oc % 2) * 1024:(oc % 2) * 1024 + 1024]
                  .rearrange("p (dc o) -> p dc o", dc=DC), in_=wo4[:, oc])
            W("pe", s_x0, em.val(s_x0))
            W("pe", s_dve)
            for oc in range(DC):
                W("pe", s_wkb, base_o + 16 * (oc + 1))
                W("pe", s_act)
                for hp in range(HP):
                    E("pe", "matmul", ps_B[:, :],
                      wkbuf[:, (oc % 2) * 1024 + hp * 128:(oc % 2) * 1024 + hp * 128 + 128],
                      mrg[:, hp * 512:hp * 512 + 512],
                      start=(hp == 0), stop=(hp == HP - 1),
                      inc=s_pe if hp == HP - 1 else None)
                if oc + 2 < DC:
                    W("pool", s_pe)
                    E("pool", "dma_start", inc=s_wkb,
                      out=wkbuf[:, (oc % 2) * 1024:(oc % 2) * 1024 + 1024]
                      .rearrange("p (dc o) -> p dc o", dc=DC),
                      in_=wo4[:, oc + 2])
                W("act", s_pe)
                W("act", s_dve)
                E("act", "activation", inc=s_act, out=kraw[:, :], in_=ps_B[:, :],
                  func=AF.Identity, scale=m_ga[:, oc:oc + 1])
                W("dve", s_act)
                W("dve", s_sp)
                E("dve", "tensor_add", x1_sb[:, oc * 512:oc * 512 + 512],
                  x0q[:, oc * 512:oc * 512 + 512], kraw[:, :], inc=s_dve)
            W("sp", s_dve)
            E("sp", "dma_start", inc=s_sp,
              out=x1d.ap().rearrange("(dc p) t -> p dc t", p=128)
              [:, :, 512 * qb:512 * qb + 512],
              in_=x1_sb[:, :].rearrange("p (dc t) -> p dc t", dc=DC))

            ln_block(x1_sb, 0, h2_sb, m_rowE2)
            for tt in range(4):
                W("dve", s_sp)
                for dc in range(DC):
                    W("pe", s_dve)
                    W("pe", s_act)
                    E("pe", "transpose", ps_tp[:, :],
                      h2_sb[:, dc * 512 + tt * 128:dc * 512 + tt * 128 + 128],
                      c_id_bf[:, :], inc=s_pe)
                    W("dve", s_pe)
                    E("dve", "tensor_copy", h2t_v[:, dc * 128:dc * 128 + 128],
                      ps_tp[:, :], inc=s_dve)
                W("sp", s_dve)
                E("sp", "dma_start", inc=s_sp,
                  out=h2d.ap()[512 * qb + 128 * tt:512 * qb + 128 * tt + 128, :],
                  in_=h2t_v[:, 0:1024])
                W("pe", s_cldg, 16)
                W("pe", s_dve)
                W("pe", s_act)
                for dc in range(DC):
                    E("pe", "matmul", B_lg,
                      h2_sb[:, dc * 512 + tt * 128:dc * 512 + tt * 128 + 128],
                      w_g[:, dc * 16:dc * 16 + 16],
                      start=(dc == 0), stop=(dc == DC - 1),
                      inc=s_pe if dc == DC - 1 else None)
                W("dve", s_pe)
                E("dve", "reduce_max", lgmax[:, :], B_lg, AX.X, negate=True,
                  inc=s_dve)
                W("act", s_dve)
                E("act", "activation", inc=s_act, out=expg[:, :], in_=B_lg,
                  func=AF.Exp, bias=lgmax[:, 0:1], accum_out=sumg[:, 0:1])
                W("dve", s_act)
                E("dve", "reciprocal", valg[:, :], sumg[:, :], inc=s_dve)
                E("dve", "reduce_max", lgmax[:, :], expg[:, :], AX.X, inc=s_dve)
                W("dve", s_dve)
                E("dve", "tensor_scalar", vig[:, :], expg[:, :], lgmax[:, 0:1],
                  None, ALU.is_ge, inc=s_dve)
                W("dve", s_dve)
                E("dve", "tensor_mul", vig[:, :], vig[:, :], c_iotam[:, :],
                  inc=s_dve)
                W("dve", s_dve)
                E("dve", "tensor_reduce", eg[:, :], vig[:, :], AX.X, ALU.min,
                  inc=s_dve)
                W("dve", s_dve)
                E("dve", "tensor_scalar", rec2[:, 1:2], eg[:, :], BIG, None,
                  ALU.add, inc=s_dve)
                E("dve", "tensor_copy", rec2[:, 0:1], valg[:, :], inc=s_dve)
                W("sp", s_dve)
                E("sp", "dma_start", inc=s_sp,
                  out=rt_in.ap()[512 * qb + 128 * tt:512 * qb + 128 * tt + 128, :],
                  in_=rec2[:, :])
            W("pool", s_dve)
            em.inc("pool", s_xfree)

        tap("x1d", x1d.ap())
        tap("h2d", h2d.ap())
        tap("rt_in", rt_in.ap())

        em.mute = LV < 3
        W("pool", s_sp)
        (em.inc("pool", s_cc) if NOAG else
         E("pool", "collective_compute", "AllGather", ALU.bypass,
           replica_groups=[list(range(NCORE))], ins=[rt_in.ap()],
           outs=[rt_out.ap()], inc=s_cc, amt=1))
        W("sp", s_pe)
        W("sp", s_dve)
        for dst, src in [
            (c_us, din["ustrict_bf"]), (c_agpos, din["agpos_f32"]),
            (c_base, din["baseB_f32"]), (c_lbase, din["localbase"]),
            (c_perm, din["permT"]), (c_i128, din["iota128_f32"]),
            (c_i20, din["iota20_f32"]), (c_iotap, din["iotap_f32"]),
        ]:
            E("sp", "dma_start", inc=s_p3c, out=dst[:, :], in_=src)
        em.mute = em.mute or KS < 2
        W("pool", s_pe)
        W("pool", s_dve)
        W("pool", s_cc, 1)
        (em.inc("pool", s_cc) if NOAG else
         E("pool", "collective_compute", "AllGather", ALU.bypass,
           replica_groups=[list(range(NCORE))], ins=[h2d.ap()],
           outs=[h2ag.ap()], inc=s_cc, amt=1))
        E("pool", "dma_start", inc=s_gx, out=g_pk[:, :],
          in_=rt_out.ap().rearrange("(p f) c -> p (f c)", p=128))
        gx_rt = em.val(s_gx)
        em.mute = em.mute or KS < 3
        E("pool", "dma_start", inc=s_wf,
          out=w1_sb[:, 0:DC * 2048].rearrange("p (dc h) -> p dc h", dc=DC),
          in_=din["w1a"].rearrange("(dc p) h -> p dc h", p=128))
        E("pool", "dma_start", inc=s_wf,
          out=w2_sb[:, :].rearrange("p (hc o) -> p hc o", hc=16),
          in_=din["w2a"].rearrange("(hc p) o -> p hc o", p=128))
        E("pool", "dma_start", inc=s_wf,
          out=w1_sb[:, DC * 2048:2 * DC * 2048]
          .rearrange("p (dc h) -> p dc h", dc=DC),
          in_=din["w1b"].rearrange("(dc p) h -> p dc h", p=128))
        E("pool", "dma_start", inc=s_cldg, out=b2rowa[:, :], in_=din["b2pa"])
        E("pool", "dma_start", inc=s_cldg, out=b2rowb[:, :], in_=din["b2pb"])

        em.mute = em.mute or KS < 4
        W("dve", s_gx, gx_rt)
        W("dve", s_p3c, em.val(s_p3c))
        pk3 = g_pk[:, :].rearrange("p (f c) -> p f c", c=2)
        E("dve", "tensor_copy", g_val[:, :].rearrange("p (f c) -> p f c", c=1),
          pk3[:, :, 0:1], inc=s_dve)
        E("dve", "tensor_copy", g_e[:, :].rearrange("p (f c) -> p f c", c=1),
          pk3[:, :, 1:2], inc=s_dve)
        W("dve", s_dve)
        for n_ in range(16):
            E("dve", "tensor_scalar",
              g_oh[:, :].rearrange("p (f i) -> p f i", i=16)[:, :, n_:n_ + 1],
              g_e[:, :].rearrange("p (f one) -> p f one", one=1),
              float(n_), None, ALU.is_equal, inc=s_dve)
        W("dve", s_dve)
        E("dve", "tensor_copy", g_pfA[:, :], g_oh[:, :], inc=s_dve)
        shift, cur, nxt = 16, g_pfA, g_pfB
        while shift < 2048:
            W("dve", s_dve)
            E("dve", "tensor_copy", nxt[:, 0:shift], cur[:, 0:shift], inc=s_dve)
            E("dve", "tensor_add", nxt[:, shift:2048], cur[:, shift:2048],
              cur[:, 0:2048 - shift], inc=s_dve)
            cur, nxt = nxt, cur
            shift *= 2
        g_incl = cur
        g_oth = g_pfA if g_incl is g_pfB else g_pfB
        W("dve", s_dve)
        E("dve", "tensor_copy", g_T[:, :], g_incl[:, 127 * 16:127 * 16 + 16],
          inc=s_dve)
        W("pe", s_dve)
        W("pe", s_act)
        E("pe", "matmul", A_offs, c_us[:, :], g_T[:, :], start=True, stop=True,
          inc=s_pe)
        W("dve", s_pe)
        E("dve", "tensor_sub", g_rank[:, :], g_incl[:, :], g_oh[:, :],
          inc=s_dve)
        W("dve", s_dve)
        E("dve", "tensor_add", g_rank[:, :].rearrange("p (f i) -> p f i", i=16),
          g_rank[:, :].rearrange("p (f i) -> p f i", i=16),
          A_offs.rearrange("p (one i) -> p one i", one=1)
          .broadcast_to((128, 128, 16)), inc=s_dve)
        W("dve", s_dve)
        E("dve", "tensor_mul", g_oth[:, :], g_rank[:, :], g_oh[:, :],
          inc=s_dve)
        W("dve", s_dve)
        E("dve", "tensor_reduce", g_rt[:, :],
          g_oth[:, :].rearrange("p (f i) -> p f i", i=16), AX.X, ALU.add,
          inc=s_dve)
        W("dve", s_dve)
        E("dve", "tensor_mul", g_oth[:, :].rearrange("p (f i) -> p f i", i=16),
          g_oh[:, :].rearrange("p (f i) -> p f i", i=16),
          c_base[:, :].rearrange("p (one i) -> p one i", one=1)
          .broadcast_to((128, 128, 16)), inc=s_dve)
        W("dve", s_dve)
        E("dve", "tensor_reduce", g_fetch[:, :],
          g_oth[:, :].rearrange("p (f i) -> p f i", i=16), AX.X, ALU.add,
          inc=s_dve)
        W("dve", s_dve)
        E("dve", "tensor_add", g_fetch[:, :], g_fetch[:, :], g_rt[:, :],
          inc=s_dve)
        E("dve", "tensor_scalar", g_sel[:, :], g_rt[:, :], float(CAP), None,
          ALU.is_ge, inc=s_dve)
        W("dve", s_dve)
        E("dve", "tensor_scalar", g_tmp[:, :], g_sel[:, :], -1.0, 1.0,
          ALU.mult, ALU.add, inc=s_dve)
        W("dve", s_dve)
        E("dve", "tensor_mul", g_fetch[:, :], g_fetch[:, :], g_tmp[:, :],
          inc=s_dve)
        E("dve", "tensor_mul", g_oth[:, :].rearrange("p (f i) -> p f i", i=16),
          g_oh[:, :].rearrange("p (f i) -> p f i", i=16),
          c_lbase[:, :].rearrange("p (one i) -> p one i", one=1)
          .broadcast_to((128, 128, 16)), inc=s_dve)
        W("dve", s_dve)
        E("dve", "tensor_reduce", g_scat[:, :],
          g_oth[:, :].rearrange("p (f i) -> p f i", i=16), AX.X, ALU.add,
          inc=s_dve)
        W("dve", s_dve)
        E("dve", "tensor_add", g_scat[:, :], g_scat[:, :], g_rt[:, :],
          inc=s_dve)
        E("dve", "tensor_scalar", g_tmp[:, :], g_sel[:, :], BIG, None,
          ALU.mult, inc=s_dve)
        W("dve", s_dve)
        E("dve", "tensor_add", g_scat[:, :], g_scat[:, :], g_tmp[:, :],
          inc=s_dve)
        E("dve", "tensor_tensor", g_lhs[:, :], c_i128[:, :],
          c_iotap[:, 0:1].broadcast_to((128, 128)), ALU.is_equal, inc=s_dve)
        for srcg, dstf in ((g_fetch, None), (g_sel, g_selT), (g_scat, g_sT)):
            W("pe", s_dve)
            W("pe", s_act)
            ident = c_perm if dstf is not g_sT else g_lhs
            E("pe", "transpose", A_tpf[:, 0:128], srcg[:, :], ident[:, :],
              inc=s_pe)
            W("dve", s_pe)
            if dstf is None:
                E("dve", "tensor_copy", g_tmp[:, :], A_tpf[:, 0:128],
                  inc=s_dve)
                W("dve", s_dve)
                E("dve", "tensor_copy", g_feU[:, :], g_tmp[:, :], inc=s_dve)
            else:
                E("dve", "tensor_copy", dstf[:, :], A_tpf[:, 0:128],
                  inc=s_dve)
        em.mute = em.mute or KS < 5
        W("dve", s_dve)
        E("dve", "tensor_scalar", g_e[:, :], g_sT[:, :], 128.0, None,
          ALU.is_ge, inc=s_dve)
        for u in range(2, 21):
            E("dve", "tensor_scalar", g_tmp[:, :], g_sT[:, :],
              float(128 * u), None, ALU.is_ge, inc=s_dve)
            E("dve", "tensor_add", g_e[:, :], g_e[:, :], g_tmp[:, :],
              inc=s_dve)
        E("dve", "tensor_scalar", g_val[:, :], g_e[:, :], -128.0, None,
          ALU.mult, inc=s_dve)
        E("dve", "tensor_add", g_val[:, :], g_val[:, :], g_sT[:, :],
          inc=s_dve)
        W("dve", s_pe)
        pe0 = em.val(s_pe)
        for k in range(128):
            lhs = g_lhs if k % 2 == 0 else g_lhs2
            rhs = g_rhs if k % 2 == 0 else g_rhs2
            if k >= 2:
                W("dve", s_pe, pe0 + k - 1)
            E("dve", "tensor_tensor", lhs[:, :], c_i128[:, :],
              g_val[:, k:k + 1].broadcast_to((128, 128)), ALU.is_equal,
              inc=s_dve)
            E("dve", "tensor_scalar", g_tsc[:, :], c_iotap[:, :],
              float(128 * k + 1), None, ALU.add, inc=s_dve)
            E("dve", "tensor_tensor", rhs[:, :], c_i20[:, :],
              g_e[:, k:k + 1].broadcast_to((128, 20)), ALU.is_equal,
              inc=s_dve)
            E("dve", "tensor_scalar", lhs[:, :], lhs[:, :], g_tsc[:, 0:1],
              None, ALU.mult, inc=s_dve)
            W("pe", s_dve)
            E("pe", "matmul", ps_B[:, 0:20], lhs[:, :], rhs[:, :],
              start=(k == 0), stop=(k == 127), inc=s_pe)
        E("dve", "memset", g_tsc[:, :], -1.0, inc=s_dve)
        W("act", s_pe)
        W("act", s_dve)
        E("act", "activation", inc=s_act, out=g_sT[:, 0:20],
          in_=ps_B[:, 0:20], func=AF.Relu, bias=g_tsc[:, 0:1])
        W("dve", s_act)
        E("dve", "tensor_copy", g_gU[:, :], g_sT[:, 0:20], inc=s_dve)
        dve_rank = em.val(s_dve)

        tap("g_rt", g_rt[:, :])
        tap("g_fetch", g_fetch[:, :])
        tap("g_gU", g_sT[:, 0:20])

        em.mute = LV < 4
        W("pool", s_cc, 2)
        W("pool", s_dve, dve_rank)
        for e2 in range(2):
            if e2 == 1:
                W("pool", s_pe)
                E("pool", "dma_start", inc=s_wf,
                  out=w2_sb[:, :].rearrange("p (hc o) -> p hc o", hc=16),
                  in_=din["w2b"].rearrange("(hc p) o -> p hc o", p=128))
            for kcall in range(10):
                col = e2 * 10 + kcall
                E("pool", "indirect_dma_start", inc=s_gx,
                  out=xe_sb[:, 1024 * kcall:1024 * kcall + 1024],
                  out_offset=None, in_=h2ag.ap(),
                  in_offset=IndirectOffsetOnAxis(ap=g_gU[:, col:col + 1],
                                                 axis=0),
                  bounds_check=BT - 1, oob_is_err=False)
                W("pool", s_gx)
            gx_done = em.val(s_gx)
            b2r = b2rowa if e2 == 0 else b2rowb
            W("pe", s_cldg)
            W("pe", s_act)
            W("pe", s_dve)
            for dcb in range(2):
                W("pe", s_act)
                E("pe", "matmul", A_bc, c_ones_r[0:1, 0:64],
                  b2r[0:1, dcb * 512:dcb * 512 + 512], start=True, stop=True,
                  inc=s_pe)
                W("act", s_pe)
                E("act", "activation", inc=s_act,
                  out=b2B[0:64, dcb * 512:dcb * 512 + 512], in_=A_bc,
                  func=AF.Identity)
            W("sp", s_act)
            E("sp", "dma_start", inc=s_modb, out=b2B[64:128, :],
              in_=b2B[0:64, :])
            spb2 = em.val(s_modb)

            for (t0, t1) in ((0, 4), (4, 8), (8, 10)):
                ng = (t1 - t0) * 128
                W("pe", s_gx, gx_done)
                W("pe", s_wf, 16 * (3 if e2 == 0 else 4))
                for dc in range(DC):
                    for tt in range(t0, t1):
                        W("pe", s_dve)
                        E("pe", "transpose", ps_tp[:, :],
                          xe_sb[:, tt * 1024 + dc * 128:tt * 1024 + dc * 128 + 128],
                          c_id_bf[:, :], inc=s_pe)
                        W("dve", s_pe)
                        E("dve", "tensor_copy",
                          xeT[:, dc * 512 + (tt - t0) * 128:
                              dc * 512 + (tt - t0) * 128 + 128],
                          ps_tp[:, :], inc=s_dve)
                W("pe", s_dve)
                for hc in range(16):
                    for dc in range(DC):
                        E("pe", "matmul", ps_A[:, 0:ng],
                          w1_sb[:, e2 * DC * 2048 + dc * 2048 + hc * 128:
                                e2 * DC * 2048 + dc * 2048 + hc * 128 + 128],
                          xeT[:, dc * 512:dc * 512 + ng],
                          start=(dc == 0), stop=(dc == DC - 1),
                          inc=s_pe if dc == DC - 1 else None)
                    W("act", s_pe)
                    E("act", "activation", inc=s_act,
                      out=hTs[:, hc * 512:hc * 512 + ng], in_=ps_A[:, 0:ng],
                      func=AF.Gelu_apprx_tanh,
                      bias=b1_sb[:, e2 * 16 + hc:e2 * 16 + hc + 1])
                    W("pe", s_act)
                W("pe", s_act)
                W("pe", s_modb, spb2)
                for tt in range(t0, t1):
                    for dcb in range(2):
                        W("pe", s_dve)
                        for hc in range(16):
                            E("pe", "matmul", ps_B[:, :],
                              hTs[:, hc * 512 + (tt - t0) * 128:
                                  hc * 512 + (tt - t0) * 128 + 128],
                              w2_sb[:, hc * 1024 + dcb * 512:
                                    hc * 1024 + dcb * 512 + 512],
                              start=(hc == 0), stop=(hc == 15),
                              inc=s_pe if hc == 15 else None)
                        W("dve", s_pe)
                        W("dve", s_sp)
                        E("dve", "tensor_add",
                          yo_sb[:, dcb * 512:dcb * 512 + 512], ps_B[:, :],
                          b2B[:, dcb * 512:dcb * 512 + 512], inc=s_dve)
                    W("sp", s_dve)
                    E("sp", "dma_start", inc=s_sp,
                      out=outblk.ap()[e2 * CAP + tt * 128:e2 * CAP + tt * 128 + 128, :],
                      in_=yo_sb[:, :])

        tap("outblk", outblk.ap())

        W("pool", s_sp)
        (em.inc("pool", s_cc) if NOAG else
         E("pool", "collective_compute", "AllGather", ALU.bypass,
           replica_groups=[list(range(NCORE))], ins=[outblk.ap()],
           outs=[outag.ap()], inc=s_cc, amt=1))

        em.mute = LV < 5
        x1d3 = x1d.ap().rearrange("(dc p) t -> p dc t", p=128)
        yout3 = yout.rearrange("(dc p) t -> p dc t", p=128)
        W("pool", s_cc, 3)
        for m in range(16):
            W("pool", s_pe)
            W("pool", s_dve)
            E("pool", "indirect_dma_start", inc=s_fin, out=z_sb[:, :],
              out_offset=None, in_=outag.ap(),
              in_offset=IndirectOffsetOnAxis(ap=g_feU[:, m:m + 1], axis=0),
              bounds_check=NCORE * 2 * CAP - 1, oob_is_err=False)
            E("pool", "dma_start", inc=s_fin, out=h2row[:, :],
              in_=h2d.ap()[128 * m:128 * m + 128, :])
            E("pool", "dma_start", inc=s_fin, out=x1r[:, :]
              .rearrange("p (dc t) -> p dc t", dc=DC),
              in_=x1d3[:, :, 128 * m:128 * m + 128])
            W("pool", s_fin)
            W("dve", s_fin, em.val(s_fin))
            E("dve", "tensor_sub", zdiff[:, :], h2row[:, :], z_sb[:, :],
              inc=s_dve)
            W("dve", s_dve)
            E("dve", "tensor_scalar", zsel[:, :], zdiff[:, :],
              g_selT[:, m:m + 1], None, ALU.mult, inc=s_dve)
            W("dve", s_dve)
            E("dve", "tensor_add", zdiff[:, :], zsel[:, :], z_sb[:, :],
              inc=s_dve)
            W("pe", s_fin, em.val(s_fin))
            W("pe", s_act)
            W("pe", s_dve)
            for dc in range(DC):
                E("pe", "transpose", ps_tp[:, :],
                  zdiff[:, dc * 128:dc * 128 + 128], c_id_bf[:, :], inc=s_pe)
                W("act", s_pe)
                W("act", s_dve)
                E("act", "activation", inc=s_act, out=zgb[:, :],
                  in_=ps_tp[:, :], func=AF.Identity, scale=m_gb[:, dc:dc + 1])
                W("dve", s_act)
                W("dve", s_out)
                E("dve", "tensor_add", yf_sb[:, dc * 128:dc * 128 + 128],
                  x1r[:, dc * 128:dc * 128 + 128], zgb[:, :], inc=s_dve)
                W("pe", s_act)
            W("sp", s_dve)
            E("sp", "dma_start", inc=s_out,
              out=yout3[:, :, 128 * m:128 * m + 128],
              in_=yf_sb[:, :].rearrange("p (dc t) -> p dc t", dc=DC))

        W("pool", s_out)
        if tap_out:
            W("pool", s_tap)

        block = es.enter_context(nc.Block())
        em.realize(block)

    return nc


_CACHE = {}


def kernel(**inputs):
    consts = _host_constants()
    if "nc" not in _CACHE:
        _CACHE["nc"] = build_program()
    nc = _CACHE["nc"]
    in_maps = []
    for c in range(NCORE):
        d = _per_core_inputs(inputs, c)
        d.update(consts)
        in_maps.append(d)
    res = run_bass_kernel_spmd(nc, in_maps, core_ids=list(range(NCORE)))
    out = np.empty((B, S, D), dtype=np.float32)
    for c in range(NCORE):
        y = np.asarray(res.results[c]["yout"])
        b, half = c // 2, c % 2
        for j in range(QB):
            g = 2 * j + half
            out[b, 512 * g:512 * g + 512, :] = y[:, 512 * j:512 * j + 512].T
    return out


# version 13 (final)
# speedup vs baseline: 1.2289x; 1.2289x over previous
import math
import numpy as np
import ml_dtypes
import concourse.bass as bass
import concourse.mybir as mybir
from concourse.bass_utils import run_bass_kernel_spmd
from concourse.bass_core_rust import IndirectOffsetOnAxis

F32 = mybir.dt.float32
BF16 = mybir.dt.bfloat16
U32 = mybir.dt.uint32
AF = mybir.ActivationFunctionType
ALU = mybir.AluOpType
AX = mybir.AxisListType

NCORE = 8
B, S, D = 4, 4096, 1024
TQ = 2048
DC = 8
H, HP, HD = 16, 8, 64
QB = 4
NE, CAP = 16, 1280
BT = B * S
EPS = 1e-5
SCALE = 1.0 / math.sqrt(HD)
BIG = 1.0e6


def _rope_cache_f32(seq_len, head_dim):
    import jax
    import jax.numpy as jnp
    with jax.default_device(jax.devices("cpu")[0]):
        pos = jnp.arange(seq_len, dtype=jnp.float32)
        freqs = 1.0 / 10000.0 ** (jnp.arange(0, head_dim, 2, dtype=jnp.float32) / head_dim)
        angles = pos[:, None] * freqs[None, :]
        cos = jnp.repeat(jnp.cos(angles), 2, axis=-1).astype(jnp.float32)
        sin = jnp.repeat(jnp.sin(angles), 2, axis=-1).astype(jnp.float32)
        return np.asarray(cos), np.asarray(sin)


def _host_constants():
    c = {}
    eye = np.eye(128, dtype=np.float32)
    c["identity_bf"] = eye.astype(ml_dtypes.bfloat16)
    c["identity_f32"] = eye
    R = np.zeros((128, 128), dtype=np.float32)
    for base in (0, 64):
        for j in range(32):
            R[base + 2 * j + 1, base + 2 * j] = -1.0
            R[base + 2 * j, base + 2 * j + 1] = 1.0
    c["rotP_bf"] = R.astype(ml_dtypes.bfloat16)
    c["ustrict_bf"] = np.triu(np.ones((128, 128), dtype=np.float32), 1) \
        .astype(ml_dtypes.bfloat16)
    kk = np.arange(128, dtype=np.float32)[:, None]
    qq = np.arange(512, dtype=np.float32)[None, :]
    c["DT_f32"] = np.ascontiguousarray(qq - kk)
    c["iotam_f32"] = np.ascontiguousarray(
        np.tile(np.arange(16, dtype=np.float32) - BIG, (128, 1)))
    grid = (np.arange(128, dtype=np.int64)[:, None] * 128
            + np.arange(128, dtype=np.int64)[None, :])
    c["agpos_f32"] = grid.astype(np.float32)
    c["iota128_f32"] = np.ascontiguousarray(
        np.tile(np.arange(128, dtype=np.float32), (128, 1)))
    c["iota20_f32"] = np.ascontiguousarray(
        np.tile(np.arange(20, dtype=np.float32), (128, 1)))
    c["iotap_f32"] = np.ascontiguousarray(
        np.arange(128, dtype=np.float32).reshape(128, 1))
    base = (np.arange(16) % 2) * (NCORE * CAP) + (np.arange(16) // 2) * CAP
    c["baseB_f32"] = np.ascontiguousarray(
        np.tile(base.astype(np.float32), (128, 1)))
    c["onesrow_bf"] = np.ones((1, 128), np.float32).astype(ml_dtypes.bfloat16)
    c["onescol_bf"] = np.ones((128, 1), np.float32).astype(ml_dtypes.bfloat16)
    return c


def _per_core_inputs(inputs, core):
    b, half = core // 2, core % 2
    x = np.asarray(inputs["L0_attn_x"])
    xb = np.ascontiguousarray(x[b].T)
    cols = np.concatenate([np.arange(512 * (2 * j + half),
                                     512 * (2 * j + half) + 512)
                           for j in range(QB)])
    d = {}
    d["xkv"] = xb
    d["xq"] = np.ascontiguousarray(xb[:, cols])
    d["wqT"] = np.ascontiguousarray(np.asarray(inputs["L0_attn_wq"]).T)
    d["wkT"] = np.ascontiguousarray(np.asarray(inputs["L0_attn_wk"]).T)
    d["wvT"] = np.ascontiguousarray(np.asarray(inputs["L0_attn_wv"]).T)
    d["woT"] = np.ascontiguousarray(np.asarray(inputs["L0_attn_wo"]).T)
    d["wmod"] = np.ascontiguousarray(np.asarray(inputs["L0_wmod"]))
    d["cfeat"] = np.ascontiguousarray(
        np.asarray(inputs["L0_c"])[b].reshape(8, 128).T)
    d["wg"] = np.ascontiguousarray(np.asarray(inputs["L0_ffn_Wg"]))
    ea, eb = 2 * core, 2 * core + 1
    W1 = np.asarray(inputs["L0_ffn_W1"])
    W2 = np.asarray(inputs["L0_ffn_W2"])
    b1 = np.asarray(inputs["L0_ffn_b1"])
    b2 = np.asarray(inputs["L0_ffn_b2"])
    d["w1a"], d["w1b"] = np.ascontiguousarray(W1[ea]), np.ascontiguousarray(W1[eb])
    d["w2a"], d["w2b"] = np.ascontiguousarray(W2[ea]), np.ascontiguousarray(W2[eb])
    d["b1p"] = np.ascontiguousarray(np.concatenate(
        [b1[e].reshape(16, 128).T for e in (ea, eb)], axis=1))
    d["b2pa"] = np.ascontiguousarray(b2[ea:ea + 1])
    d["b2pb"] = np.ascontiguousarray(b2[eb:eb + 1])
    cos, sin = _rope_cache_f32(S, HD)
    d["cosk"] = np.ascontiguousarray(np.tile(cos.T, (2, 1)))
    d["sink"] = np.ascontiguousarray(np.tile(sin.T, (2, 1)))
    d["cosq"] = np.ascontiguousarray(np.tile(cos[cols].T, (2, 1)))
    d["sinq"] = np.ascontiguousarray(np.tile(sin[cols].T, (2, 1)))
    d["thetas"] = np.ascontiguousarray(np.tile(
        (128.0 * np.arange(8) - 512.0 * half).astype(np.float32), (128, 1)))
    lb = np.full(16, BIG, np.float32)
    lb[ea], lb[eb] = 0.0, float(CAP)
    d["localbase"] = np.ascontiguousarray(np.tile(lb, (128, 1)))
    P = np.zeros((128, 128), dtype=np.float32)
    mine = [32 * (m // 4) + 4 * core + (m % 4) for m in range(16)]
    order = mine + [r for r in range(128) if r not in mine]
    for newcol, row in enumerate(order):
        P[row, newcol] = 1.0
    d["permT"] = P
    return d


class Emitter:
    ENGS = ("pe", "act", "dve", "pool", "sp")

    def __init__(self, nc):
        self.nc = nc
        self.q = {e: [] for e in self.ENGS}
        self.cnt = {}
        self.mute = False

    def val(self, sem):
        return self.cnt.get(sem.name, 0)

    def op(self, eng, fn, *args, inc=None, amt=None, lp=False, **kw):
        if self.mute:
            return 0
        if inc is not None:
            amt = amt if amt is not None else (16 if "dma" in fn else 1)
            self.cnt[inc.name] = self.cnt.get(inc.name, 0) + amt

        def run(e, fn=fn, args=args, kw=kw, inc=inc, amt=amt, lp=lp):
            if lp:
                with self.nc.allow_low_precision("bf16 rounding within tolerance"):
                    r = getattr(e, fn)(*args, **kw)
            else:
                r = getattr(e, fn)(*args, **kw)
            if inc is not None:
                r.then_inc(inc, amt)
        self.q[eng].append(run)
        return self.cnt.get(inc.name, 0) if inc is not None else None

    def wait(self, eng, sem, val=None):
        if self.mute:
            return
        v = self.val(sem) if val is None else val
        self.q[eng].append(lambda e, s=sem, v=v: e.wait_ge(s, v))

    def inc(self, eng, sem, amt=1):
        if self.mute:
            return 0
        self.cnt[sem.name] = self.cnt.get(sem.name, 0) + amt
        v = self.cnt[sem.name]
        self.q[eng].append(lambda e, s=sem, a=amt: e.sem_inc(s, a))
        return v

    def realize(self, block):
        q = self.q

        @block.tensor
        def _(e):
            for f in q["pe"]:
                f(e)

        @block.scalar
        def _(e):
            for f in q["act"]:
                f(e)

        @block.vector
        def _(e):
            for f in q["dve"]:
                f(e)

        @block.gpsimd
        def _(e):
            for f in q["pool"]:
                f(e)

        @block.sync
        def _(e):
            for f in q["sp"]:
                f(e)


def build_program(taps=()):
    import os
    LV = int(os.environ.get("KLV", "99"))
    KS = int(os.environ.get("KSUB", "99"))
    NOAG = os.environ.get("KNOAG", "0") == "1"
    nc = bass.Bass("TRN2", target_bir_lowering=False,
                   detect_race_conditions=False)

    din = {}
    for name, shape in [
        ("xkv", [D, S]), ("xq", [D, TQ]),
        ("wqT", [D, D]), ("wkT", [D, D]), ("wvT", [D, D]), ("woT", [D, D]),
        ("wmod", [D, 6 * D]), ("cfeat", [128, 8]), ("wg", [D, 16]),
        ("w1a", [D, 2048]), ("w1b", [D, 2048]),
        ("w2a", [2048, D]), ("w2b", [2048, D]),
        ("b1p", [128, 32]), ("b2pa", [1, D]), ("b2pb", [1, D]),
        ("cosk", [128, S]), ("sink", [128, S]),
        ("cosq", [128, TQ]), ("sinq", [128, TQ]),
        ("thetas", [128, 8]), ("localbase", [128, 16]),
        ("DT_f32", [128, 512]), ("iotam_f32", [128, 16]),
        ("agpos_f32", [128, 128]), ("baseB_f32", [128, 16]),
        ("permT", [128, 128]), ("identity_f32", [128, 128]),
    ]:
        din[name] = nc.dram_tensor(name, shape, F32, kind="ExternalInput").ap()
    for name in ("identity_bf", "rotP_bf", "ustrict_bf"):
        din[name] = nc.dram_tensor(name, [128, 128], BF16,
                                   kind="ExternalInput").ap()
    din["onesrow_bf"] = nc.dram_tensor("onesrow_bf", [1, 128], BF16,
                                       kind="ExternalInput").ap()
    din["onescol_bf"] = nc.dram_tensor("onescol_bf", [128, 1], BF16,
                                       kind="ExternalInput").ap()
    for name, shape in [("iota128_f32", [128, 128]), ("iota20_f32", [128, 20]),
                        ("iotap_f32", [128, 1])]:
        din[name] = nc.dram_tensor(name, shape, F32, kind="ExternalInput").ap()

    yout = nc.dram_tensor("yout", [D, TQ], F32, kind="ExternalOutput").ap()
    tap_out = {}
    for tname, tshape, tdt in taps:
        tap_out[tname] = nc.dram_tensor("tap_" + tname, tshape, tdt,
                                        kind="ExternalOutput").ap()

    rt_in = nc.dram_tensor("rt_in", [TQ, 2], F32)
    rt_out = nc.dram_tensor("rt_out", [BT, 2], F32)
    h2d = nc.dram_tensor("h2d", [TQ, D], BF16)
    h2ag = nc.dram_tensor("h2ag", [BT, D], BF16)
    x1d = nc.dram_tensor("x1d", [D, TQ], BF16)
    modsc = nc.dram_tensor("modsc", [1, 1024], F32)
    outblk = nc.dram_tensor("outblk", [2 * CAP, D], BF16)
    outag = nc.dram_tensor("outag", [NCORE * 2 * CAP, D], BF16)

    SBUF_BASE = 17408
    SBUF_LIMIT = 229344
    cursor = [SBUF_BASE]
    offs = {}

    def _bytes(shape, dt):
        per = {F32: 4, BF16: 2, U32: 4}[dt]
        free = 1
        for s_ in shape[1:]:
            free *= s_
        return free * per

    def alloc(name, shape, dt, at=None):
        if at is None:
            off = (cursor[0] + 31) // 32 * 32
            cursor[0] = off + _bytes(shape, dt)
        else:
            off = at
        assert off + _bytes(shape, dt) <= SBUF_LIMIT, (name, off)
        offs[name] = off
        return nc.alloc_sbuf_tensor_at(name, shape, dt, offset=off)

    c_id_bf = alloc("c_id_bf", [128, 128], BF16)
    c_rot = alloc("c_rot", [128, 128], BF16)
    c_ones_r = alloc("c_ones_r", [1, 128], BF16)
    c_ones_c = alloc("c_ones_c", [128, 1], BF16)
    c_theta = alloc("c_theta", [128, 8], F32)
    c_iotam = alloc("c_iotam", [128, 16], F32)
    m_c = alloc("m_c", [128, 8], F32)
    m_silu = alloc("m_silu", [128, 8], BF16)
    m_ga = alloc("m_ga", [128, 8], F32)
    m_gb = alloc("m_gb", [128, 8], F32)
    m_sc1 = alloc("m_sc1", [128, 8], F32)
    m_sc2 = alloc("m_sc2", [128, 8], F32)
    m_t8 = alloc("m_t8", [8, 128], F32)
    m_r1 = alloc("m_r1", [1, 1024], BF16)
    m_rowE = alloc("m_rowE", [2, 1024], BF16)
    m_rowE2 = alloc("m_rowE2", [2, 1024], BF16)
    w_g = alloc("w_g", [128, DC * 16], BF16)
    b1_sb = alloc("b1_sb", [128, 32], F32)
    r_rows = alloc("r_rows", [1, 1536], F32)
    r_bf = alloc("r_bf", [128, 512], BF16)
    s_recip_t = alloc("s_recip_t", [1, 512], BF16)
    r_F = alloc("r_F", [2, 512], BF16)
    xsq = alloc("xsq", [128, 512], BF16)
    kraw = alloc("kraw", [128, 512], BF16)
    kt1 = alloc("kt1", [128, 512], BF16)
    kt2 = alloc("kt2", [128, 512], BF16)
    s_bcast = alloc("s_bcast", [64, 512], BF16)
    lgmax = alloc("lgmax", [128, 1], F32)
    expg = alloc("expg", [128, 16], F32)
    sumg = alloc("sumg", [128, 1], F32)
    valg = alloc("valg", [128, 1], F32)
    vig = alloc("vig", [128, 16], F32)
    eg = alloc("eg", [128, 1], F32)
    rec2 = alloc("rec2", [128, 2], F32)
    idxg = alloc("idxg", [128, 1], U32)
    zgb = alloc("zgb", [128, 128], BF16)
    g_feU = alloc("g_feU", [128, 128], U32)
    g_selT = alloc("g_selT", [128, 128], F32)
    g_gU = alloc("g_gU", [128, 20], U32)

    x_sb = alloc("x_sb", [128, DC * 512], BF16)
    h1_sb = alloc("h1_sb", [128, DC * 512], BF16)
    qT = alloc("qT", [128, DC * 512], BF16)
    mrg = alloc("mrg", [128, HP * 512], BF16)
    expT = alloc("expT", [128, 2048], BF16)
    wvq = alloc("wvq", [128, DC * 256], BF16)
    ropeA = alloc("ropeA", [128, 512], BF16)
    ropeB = alloc("ropeB", [128, 512], BF16)
    wkbuf = alloc("wkbuf", [128, 2 * 1024], BF16)
    mask8 = alloc("mask8", [128, 8 * 512], BF16)
    kT = alloc("kT", [128, DC * S], BF16)
    v_sb = alloc("v_sb", [128, 32 * 1040], BF16)

    x0q = h1_sb
    x1_sb = qT
    h2_sb = mrg
    h2t_v = expT
    big1, big2 = offs["kT"], offs["v_sb"]
    c_dt = alloc("c_dt", [128, 512], F32, at=offs["wkbuf"])
    m_wch = alloc("m_wch", [128, 2 * 512], BF16, at=offs["wvq"])
    m_row = alloc("m_row", [1, 6 * D], F32, at=big2)
    c_idf = alloc("c_idf", [128, 128], F32, at=big2 + 24576)
    g_oh = alloc("g_oh", [128, 2048], F32, at=big2)
    g_pfA = alloc("g_pfA", [128, 2048], F32, at=big2 + 8192)
    g_pfB = alloc("g_pfB", [128, 2048], F32, at=big2 + 16384)
    g_rank = alloc("g_rank", [128, 2048], F32, at=big2 + 24576)
    g_pk = alloc("g_pk", [128, 256], F32, at=big2 + 32768)
    g_val = alloc("g_val", [128, 128], F32, at=big2 + 33792)
    g_e = alloc("g_e", [128, 128], F32, at=big2 + 34304)
    g_T = alloc("g_T", [128, 16], BF16, at=big2 + 34816)
    g_rt = alloc("g_rt", [128, 128], F32, at=big2 + 34848)
    g_fetch = alloc("g_fetch", [128, 128], F32, at=big2 + 35360)
    g_fb = alloc("g_fb", [128, 128], F32, at=big2 + 35872)
    g_scat = alloc("g_scat", [128, 128], F32, at=big2 + 36384)
    g_tmp = alloc("g_tmp", [128, 128], F32, at=big2 + 36896)
    g_sel = alloc("g_sel", [128, 128], F32, at=big2 + 37408)
    c_us = alloc("c_us", [128, 128], BF16, at=big2 + 37920)
    c_agpos = alloc("c_agpos", [128, 128], F32, at=big2 + 38176)
    c_base = alloc("c_base", [128, 16], F32, at=big2 + 38688)
    c_lbase = alloc("c_lbase", [128, 16], F32, at=big2 + 38752)
    c_perm = alloc("c_perm", [128, 128], F32, at=big2 + 38816)
    c_i128 = alloc("c_i128", [128, 128], F32, at=big2 + 39328 + 192)
    c_i20 = alloc("c_i20", [128, 20], F32, at=big2 + 39328 + 192 + 512)
    c_iotap = alloc("c_iotap", [128, 1], F32, at=big2 + 39328)
    g_sT = alloc("g_sT", [128, 128], F32, at=big2 + 40128)
    g_lhs = alloc("g_lhs", [128, 128], F32, at=big2 + 40640)
    g_rhs = alloc("g_rhs", [128, 20], F32, at=big2 + 41152)
    g_tsc = alloc("g_tsc", [128, 1], F32, at=big2 + 41248)
    g_lhs2 = alloc("g_lhs2", [128, 128], F32, at=big2 + 24576)
    g_rhs2 = alloc("g_rhs2", [128, 20], F32, at=big2 + 25088)
    g_selB = alloc("g_selB", [128, 128], BF16, at=big2 + 25184)
    w1_sb = alloc("w1_sb", [128, 2 * DC * 2048], BF16, at=big1)
    xe_sb = alloc("xe_sb", [128, 10 * 1024], BF16, at=big2 + 41312)
    hTs = alloc("hTs", [128, 16 * 512], BF16, at=big2 + 8192)
    w2_sb = alloc("w2_sb", [128, 16 * 1024], BF16, at=offs["x_sb"])
    xeT = alloc("xeT", [128, DC * 512], BF16, at=offs["expT"])
    b2B = alloc("b2B", [128, 1024], BF16, at=offs["ropeA"])
    b2rowa = alloc("b2rowa", [1, 1024], BF16, at=big2 + 62048)
    b2rowb = alloc("b2rowb", [1, 1024], BF16, at=big2 + 64096)
    yf_sb = alloc("yf_sb", [128, 1024], F32, at=offs["wkbuf"])
    z_sb = alloc("z_sb", [128, 1024], BF16, at=offs["mask8"])
    x1r = alloc("x1r", [128, 1024], BF16, at=offs["mask8"] + 2048)
    yo_sb = alloc("yo_sb", [128, 1024], BF16, at=offs["mask8"] + 4096)
    h2row = alloc("h2row", [128, 1024], BF16, at=offs["mask8"] + 4096)
    zsel = alloc("zsel", [128, 1024], BF16, at=offs["mask8"] + 6144)
    zdiff = alloc("zdiff", [128, 1024], BF16, at=offs["ropeA"])
    z_sbB = alloc("z_sbB", [128, 1024], BF16, at=offs["expT"])
    h2rowB = alloc("h2rowB", [128, 1024], BF16, at=offs["expT"] + 2048)
    x1rB = alloc("x1rB", [128, 1024], BF16, at=offs["expT"] + 4096)
    zdiffB = alloc("zdiffB", [128, 1024], BF16, at=offs["expT"] + 6144)
    zselB = alloc("zselB", [128, 1024], BF16, at=offs["h1_sb"])

    assert cursor[0] <= SBUF_LIMIT, cursor[0]
    assert offs["v_sb"] + 66560 <= SBUF_LIMIT, offs["v_sb"]

    r_mean = r_rows[0:1, 0:512]
    r_ex2 = r_rows[0:1, 512:1024]
    r_var = r_rows[0:1, 1024:1536]
    r_rstd = r_bf[0:1, :]
    s_recip = s_recip_t[0:1, :]

    from contextlib import ExitStack
    es = ExitStack()
    psa = lambda n_, s_, d_=F32: es.enter_context(nc.psum_tensor(n_, s_, d_))
    sem = lambda n_: es.enter_context(nc.semaphore(n_))

    with es:
        ps_A = psa("ps_A", [128, 2048])
        ps_B = psa("ps_B", [128, 512])
        ps_o65a = psa("ps_o65a", [65, 512])
        ps_o65b = psa("ps_o65b", [65, 512])
        ps_tp = psa("ps_tp", [128, 128], BF16)
        A_E = ps_A[:, 0:512]
        A_F = ps_A[:, 512:1024]
        A_s1 = ps_A[0:1, 1024:1536]
        A_s2 = ps_A[0:1, 1536:2048]
        A_rot = ps_A[:, 1024:1536]
        A_bc = ps_A[0:64, 0:512]
        A_mod = ps_A[0:1, 0:512]
        A_offs = ps_A[:, 0:16]
        A_tpf = ps_A[:, 0:128]
        B_lg = ps_B[:, 0:16]

        s_cld = sem("s_cld")
        s_cldg = sem("s_cldg")
        s_wmod = sem("s_wmod")
        s_modb = sem("s_modb")
        s_wkb = sem("s_wkb")
        s_whalf = sem("s_whalf")
        s_rope = sem("s_rope")
        s_xin = sem("s_xin")
        s_xfree = sem("s_xfree")
        s_x0 = sem("s_x0")
        s_p3c = sem("s_p3c")
        s_pe = sem("s_pe")
        s_act = sem("s_act")
        s_dve = sem("s_dve")
        s_sp = sem("s_sp")
        s_cc = sem("s_cc")
        s_wf = sem("s_wf")
        s_gx = sem("s_gx")
        s_out = sem("s_out")
        s_fin = sem("s_fin")
        s_tap = sem("s_tap")

        em = Emitter(nc)
        E = em.op
        W = em.wait

        def tap(name, src_ap):
            if name in tap_out:
                W("sp", s_dve)
                W("sp", s_act)
                W("sp", s_pe)
                E("sp", "dma_start", inc=s_tap, out=tap_out[name], in_=src_ap)

        for dst, src in [
            (c_theta, din["thetas"]), (c_iotam, din["iotam_f32"]),
            (c_id_bf, din["identity_bf"]), (c_rot, din["rotP_bf"]),
            (c_ones_r, din["onesrow_bf"]), (c_ones_c, din["onescol_bf"]),
            (m_c, din["cfeat"]), (b1_sb, din["b1p"]),
            (c_dt, din["DT_f32"]), (c_idf, din["identity_f32"]),
        ]:
            E("sp", "dma_start", inc=s_cld, out=dst[:, :], in_=src)
        E("pool", "dma_start", inc=s_cldg,
          out=w_g[:, :].rearrange("p (dc e) -> p dc e", dc=DC),
          in_=din["wg"].rearrange("(dc p) e -> p dc e", p=128))

        W("dve", s_cld)
        for w in range(8):
            E("dve", "tensor_scalar", mask8[:, 512 * w:512 * w + 512],
              c_dt[:, :], c_theta[:, w:w + 1], None, ALU.is_ge, inc=s_dve)
        E("dve", "memset", r_F[:, :], 1.0, inc=s_dve)

        W("act", s_cld)
        E("act", "activation", inc=s_act, out=m_silu[:, :], in_=m_c[:, :],
          func=AF.Silu)
        act_silu = em.val(s_act)
        NMOD = 12
        for part in range(NMOD):
            for dc in range(DC):
                it = part * DC + dc
                if it >= 2:
                    W("pool", s_pe, it - 1)
                E("pool", "dma_start", inc=s_wmod,
                  out=m_wch[:, 512 * (it % 2):512 * (it % 2) + 512],
                  in_=din["wmod"][128 * dc:128 * dc + 128,
                                  512 * part:512 * part + 512])
        W("pe", s_act, act_silu)
        W("pe", s_cld)
        for part in range(NMOD):
            for dc in range(DC):
                it = part * DC + dc
                W("pe", s_wmod, 16 * (it + 1))
                E("pe", "matmul", A_mod, m_silu[:, dc:dc + 1],
                  m_wch[:, 512 * (it % 2):512 * (it % 2) + 512],
                  start=(dc == 0), stop=(dc == DC - 1), inc=s_pe)
            W("act", s_pe)
            E("act", "activation", inc=s_act,
              out=m_row[:, 512 * part:512 * part + 512], in_=A_mod,
              func=AF.Identity)
        act_modrow = em.val(s_act)

        W("dve", s_act, act_modrow)
        E("dve", "tensor_scalar", m_rowE[0:1, :], m_row[:, 1024:2048], 1.0,
          None, ALU.add, inc=s_dve)
        E("dve", "tensor_scalar", m_rowE2[0:1, :], m_row[:, 4096:5120], 1.0,
          None, ALU.add, inc=s_dve)
        E("dve", "tensor_scalar", m_r1[:, :], m_row[:, 0:1024], -1.0,
          None, ALU.mult, inc=s_dve)
        W("sp", s_dve)
        E("sp", "dma_start", inc=s_modb, out=m_rowE[1:2, :], in_=m_r1[:, :])
        W("dve", s_modb, em.val(s_modb))
        E("dve", "tensor_scalar", m_r1[:, :], m_row[:, 3072:4096], -1.0,
          None, ALU.mult, inc=s_dve)
        W("sp", s_dve)
        E("sp", "dma_start", inc=s_modb, out=m_rowE2[1:2, :], in_=m_r1[:, :])

        for off, dstcol in ((2048, m_ga), (5120, m_gb)):
            W("sp", s_act, act_modrow)
            W("sp", s_pe)
            E("sp", "dma_start", inc=s_modb, out=modsc.ap(),
              in_=m_row[0:1, off:off + 1024])
            W("sp", s_modb)
            E("sp", "dma_start", inc=s_modb, out=m_t8[:, :],
              in_=modsc.ap().rearrange("one (a bq) -> a (one bq)", a=8))
            W("pe", s_modb)
            E("pe", "transpose", A_tpf[:, 0:8], m_t8[:, :], c_idf[0:8, 0:8],
              inc=s_pe)
            W("act", s_pe)
            W("act", s_dve)
            E("act", "activation", inc=s_act, out=m_sc1[:, :],
              in_=A_tpf[:, 0:8], func=AF.Sign)
            E("act", "activation", inc=s_act, out=m_sc2[:, :],
              in_=A_tpf[:, 0:8], func=AF.Abs)
            W("dve", s_act)
            E("dve", "tensor_scalar", m_sc2[:, :], m_sc2[:, :], 0.25, None,
              ALU.max, inc=s_dve)
            W("dve", s_dve)
            E("dve", "tensor_mul", dstcol[:, :], m_sc1[:, :], m_sc2[:, :],
              inc=s_dve)
        act_mod_done = em.val(s_act)
        dve_mod_done = em.val(s_dve)

        W("dve", s_modb, em.val(s_modb))
        E("dve", "memset", v_sb[:, :], 1.0, inc=s_dve)
        dve_vones = em.val(s_dve)

        em.mute = LV < 1
        xkv3 = din["xkv"].rearrange("(dc p) t -> p dc t", p=128)
        xq3 = din["xq"].rearrange("(dc p) t -> p dc t", p=128)
        wk4 = din["wkT"].rearrange("(dc p) (oc o) -> p oc dc o", p=128, o=128)
        wv4 = din["wvT"].rearrange("(dc p) (dvq o) -> p dvq dc o", p=128, o=256)
        wq4 = din["wqT"].rearrange("(dc p) (oc o) -> p oc dc o", p=128, o=128)
        wo4 = din["woT"].rearrange("(dc p) (oc o) -> p oc dc o", p=128, o=128)

        def ln_block(src, srcoff, dst, rows):
            for dc in range(DC):
                o = srcoff + dc * 512
                W("dve", s_pe)
                E("dve", "tensor_mul", xsq[:, :], src[:, o:o + 512],
                  src[:, o:o + 512], inc=s_dve)
                W("pe", s_dve)
                E("pe", "matmul", A_s1, c_ones_c[:, 0:1], src[:, o:o + 512],
                  start=(dc == 0), stop=(dc == DC - 1))
                E("pe", "matmul", A_s2, c_ones_c[:, 0:1], xsq[:, :],
                  start=(dc == 0), stop=(dc == DC - 1), inc=s_pe)
            W("act", s_pe)
            W("act", s_dve)
            E("act", "activation", inc=s_act, out=r_mean, in_=A_s1,
              func=AF.Identity, scale=1.0 / D)
            E("act", "activation", inc=s_act, out=r_ex2, in_=A_s2,
              func=AF.Identity, scale=1.0 / D)
            W("dve", s_act)
            E("dve", "tensor_mul", r_var, r_mean, r_mean, inc=s_dve)
            W("dve", s_dve)
            E("dve", "tensor_sub", r_var, r_ex2, r_var, inc=s_dve)
            W("dve", s_dve)
            E("dve", "tensor_scalar", r_var, r_var, EPS, None, ALU.add,
              inc=s_dve)
            W("act", s_dve)
            E("act", "activation", inc=s_act, out=r_var, in_=r_var,
              func=AF.Ln)
            W("act", s_act)
            E("act", "activation", inc=s_act, out=r_rstd, in_=r_var,
              func=AF.Exp, scale=-0.5)
            W("dve", s_act)
            E("dve", "tensor_mul", r_F[0:1, :], r_mean, r_rstd, inc=s_dve)
            for dc in range(DC):
                o = srcoff + dc * 512
                W("pe", s_dve)
                W("pe", s_act)
                E("pe", "matmul", A_E, rows[0:1, 128 * dc:128 * dc + 128],
                  r_rstd, start=True, stop=True)
                E("pe", "matmul", A_F, rows[:, 128 * dc:128 * dc + 128],
                  r_F[:, :], start=True, stop=True, inc=s_pe)
                W("dve", s_pe)
                E("dve", "tensor_mul", dst[:, dc * 512:dc * 512 + 512],
                  src[:, o:o + 512], A_E, inc=s_dve)
                W("dve", s_dve)
                E("dve", "tensor_sub", dst[:, dc * 512:dc * 512 + 512],
                  dst[:, dc * 512:dc * 512 + 512], A_F, inc=s_dve)

        def stream_rope(cos_src, sin_src, coloff):
            W("pool", s_dve)
            E("pool", "dma_start", inc=s_rope, out=ropeA[:, :],
              in_=cos_src[:, coloff:coloff + 512])
            E("pool", "dma_start", inc=s_rope, out=ropeB[:, :],
              in_=sin_src[:, coloff:coloff + 512])

        def proj_stream(w4, dst, dstoff, dststride):
            base = em.val(s_wkb)
            for oc in (0, 1):
                W("pool", s_pe)
                W("pool", s_dve)
                E("pool", "dma_start", inc=s_wkb,
                  out=wkbuf[:, (oc % 2) * 1024:(oc % 2) * 1024 + 1024]
                  .rearrange("p (dc o) -> p dc o", dc=DC), in_=w4[:, oc])
            W("dve", s_rope, em.val(s_rope))
            evac_d, kt1_d, kt2_d, rot_d, mm_d = {}, {}, {}, {}, {}

            def p_mm(oc):
                pp = ps_B[:, :] if oc % 2 == 0 else ps_A[:, 0:512]
                W("pe", s_wkb, base + 16 * (oc + 1))
                if oc >= 2:
                    W("pe", s_act, evac_d[oc - 2])
                else:
                    W("pe", s_act)
                    W("pe", s_dve)
                for dc in range(DC):
                    E("pe", "matmul", pp,
                      wkbuf[:, (oc % 2) * 1024 + dc * 128:(oc % 2) * 1024 + dc * 128 + 128],
                      h1_sb[:, dc * 512:dc * 512 + 512],
                      start=(dc == 0), stop=(dc == DC - 1),
                      inc=s_pe if dc == DC - 1 else None)
                mm_d[oc] = em.val(s_pe)
                if oc + 2 < DC:
                    W("pool", s_pe, mm_d[oc - 0])
                    E("pool", "dma_start", inc=s_wkb,
                      out=wkbuf[:, (oc % 2) * 1024:(oc % 2) * 1024 + 1024]
                      .rearrange("p (dc o) -> p dc o", dc=DC),
                      in_=w4[:, oc + 2])
                kb = kraw if oc % 2 == 0 else xsq
                W("act", s_pe, mm_d[oc])
                if oc >= 2:
                    W("act", s_dve, kt1_d[oc - 2])
                E("act", "activation", inc=s_act, out=kb[:, :], in_=pp,
                  func=AF.Identity)
                evac_d[oc] = em.val(s_act)

            def p_tail(oc):
                pp = ps_B[:, :] if oc % 2 == 0 else ps_A[:, 0:512]
                kb = kraw if oc % 2 == 0 else xsq
                rp = A_rot if oc % 2 == 0 else ps_A[:, 1536:2048]
                W("pe", s_act, evac_d[oc])
                if oc >= 2:
                    W("pe", s_dve, kt2_d[oc - 2])
                E("pe", "matmul", rp, c_rot[:, :], kb[:, :], start=True,
                  stop=True, inc=s_pe)
                rot_d[oc] = em.val(s_pe)
                W("dve", s_act, evac_d[oc])
                E("dve", "tensor_mul", kt1[:, :], kb[:, :], ropeA[:, :],
                  inc=s_dve)
                kt1_d[oc] = em.val(s_dve)
                W("dve", s_pe, rot_d[oc])
                E("dve", "tensor_mul", kt2[:, :], rp, ropeB[:, :],
                  inc=s_dve)
                kt2_d[oc] = em.val(s_dve)
                E("dve", "tensor_add",
                  dst[:, oc * dststride + dstoff:oc * dststride + dstoff + 512],
                  kt1[:, :], kt2[:, :], inc=s_dve)

            for oc in range(DC):
                p_mm(oc)
                if oc >= 1:
                    p_tail(oc - 1)
            p_tail(DC - 1)

        for tb in range(8):
            if tb >= 1:
                W("pool", s_xfree, tb)
            E("pool", "dma_start", inc=s_xin,
              out=x_sb[:, :].rearrange("p (dc t) -> p dc t", dc=DC),
              in_=xkv3[:, :, 512 * tb:512 * tb + 512])
            W("dve", s_xin, 16 * (tb + 1))
            if tb == 0:
                W("dve", s_dve, dve_vones)
                W("dve", s_act, act_mod_done)
            ln_block(x_sb, 0, h1_sb, m_rowE)
            stream_rope(din["cosk"], din["sink"], 512 * tb)
            proj_stream(wk4, kT, 512 * tb, S)
            W("pe", s_act)
            vcopy_done = {}
            for dvq in range(4):
                W("pool", s_pe)
                E("pool", "dma_start", inc=s_whalf,
                  out=wvq[:, :].rearrange("p (dc o) -> p dc o", dc=DC),
                  in_=wv4[:, dvq])
                W("pe", s_whalf, em.val(s_whalf))
                for tt in range(4):
                    ti2 = dvq * 4 + tt
                    vps = ps_B[:, 0:256] if tt % 2 == 0 else ps_A[:, 0:256]
                    if ti2 >= 2:
                        W("pe", s_dve, vcopy_done[ti2 - 2])
                    for dc in range(DC):
                        E("pe", "matmul", vps,
                          h1_sb[:, dc * 512 + tt * 128:dc * 512 + tt * 128 + 128],
                          wvq[:, dc * 256:dc * 256 + 256],
                          start=(dc == 0), stop=(dc == DC - 1),
                          inc=s_pe if dc == DC - 1 else None)
                    ti = tb * 4 + tt
                    W("dve", s_pe)
                    dstv = v_sb[:, ti * 1040 + dvq * 260:ti * 1040 + dvq * 260 + 260] \
                        .rearrange("p (h c) -> p h c", h=4)[:, :, 0:64]
                    srcv = vps.rearrange("p (h c) -> p h c", h=4)
                    vcopy_done[ti2] = E("dve", "tensor_copy", dstv, srcv, inc=s_dve)
                W("pe", s_dve)
            W("pool", s_dve)
            em.inc("pool", s_xfree)

        em.mute = LV < 2
        for qb in range(QB):
            W("pool", s_xfree, 8 + qb)
            E("pool", "dma_start", inc=s_xin,
              out=x_sb[:, :].rearrange("p (dc t) -> p dc t", dc=DC),
              in_=xq3[:, :, 512 * qb:512 * qb + 512])
            W("dve", s_xin, 16 * (9 + qb))
            ln_block(x_sb, 0, h1_sb, m_rowE)
            stream_rope(din["cosq"], din["sinq"], 512 * qb)
            proj_stream(wq4, qT, 0, 512)
            W("pool", s_pe)
            E("pool", "dma_start", inc=s_x0,
              out=x0q[:, :].rearrange("p (dc t) -> p dc t", dc=DC),
              in_=xq3[:, :, 512 * qb:512 * qb + 512])

            ktiles = 8 * qb + 8
            for hp in range(HP):
                if hp == 0:
                    W("act", s_sp)
                exp_done = {}
                msk_done = {}
                av_done = {}
                qk_done = {}

                def emit_qk(kt):
                    half = kt % 2
                    if kt >= 2:
                        W("pe", s_act, exp_done[kt - 2])
                    else:
                        W("pe", s_act)
                        W("pe", s_dve)
                    for h01 in range(2):
                        reg = (half * 2 + h01) * 512
                        E("pe", "matmul", ps_A[:, reg:reg + 512],
                          kT[64 * h01:64 * h01 + 64,
                             hp * S + kt * 128:hp * S + kt * 128 + 128],
                          qT[64 * h01:64 * h01 + 64, hp * 512:hp * 512 + 512],
                          start=True, stop=True,
                          tile_position=(64 * h01, 0),
                          inc=s_pe if h01 == 1 else None)
                    qk_done[kt] = em.val(s_pe)
                    wv = qk_done[kt]
                    if kt >= 2:
                        wv = max(wv, av_done[kt - 2])
                    W("act", s_pe, wv)
                    half_off = half * 1024
                    E("act", "activation", inc=s_act,
                      out=expT[:, half_off:half_off + 1024],
                      in_=ps_A[:, half_off:half_off + 1024], func=AF.Exp,
                      scale=SCALE)
                    exp_done[kt] = em.val(s_act)
                    w0 = kt - (ktiles - 8)
                    if w0 >= 0:
                        W("dve", s_act, exp_done[kt])
                        mk = mask8[:, w0 * 512:w0 * 512 + 512] \
                            .rearrange("p (one t) -> p one t", one=1) \
                            .broadcast_to((128, 2, 512))
                        E("dve", "tensor_mul",
                          expT[:, half_off:half_off + 1024]
                          .rearrange("p (h t) -> p h t", h=2),
                          expT[:, half_off:half_off + 1024]
                          .rearrange("p (h t) -> p h t", h=2), mk,
                          inc=s_dve)
                        msk_done[kt] = em.val(s_dve)
                    else:
                        msk_done[kt] = None

                def emit_av(kt):
                    half = kt % 2
                    if msk_done[kt] is not None:
                        W("pe", s_dve, msk_done[kt])
                    else:
                        W("pe", s_act, exp_done[kt])
                    for h01 in range(2):
                        dst65 = ps_o65a if h01 == 0 else ps_o65b
                        E("pe", "matmul", dst65[:, :],
                          v_sb[:, kt * 1040 + (2 * hp + h01) * 65:
                               kt * 1040 + (2 * hp + h01) * 65 + 65],
                          expT[:, (half * 2 + h01) * 512:
                               (half * 2 + h01) * 512 + 512],
                          start=(kt == 0),
                          stop=(kt == ktiles - 1),
                          inc=s_pe if h01 == 1 else None)
                    av_done[kt] = em.val(s_pe)

                for kt in range(ktiles):
                    emit_qk(kt)
                    if kt >= 1:
                        emit_av(kt - 1)
                emit_av(ktiles - 1)
                for h01 in range(2):
                    src65 = ps_o65a if h01 == 0 else ps_o65b
                    W("dve", s_pe)
                    E("dve", "reciprocal", s_recip, src65[64:65, :], inc=s_dve, lp=True)
                    W("pe", s_dve)
                    W("pe", s_act)
                    E("pe", "matmul", A_bc, c_ones_r[0:1, 0:64], s_recip,
                      start=True, stop=True, inc=s_pe)
                    W("act", s_pe)
                    W("act", s_dve)
                    E("act", "activation", inc=s_act, out=s_bcast[:, :],
                      in_=A_bc, func=AF.Identity)
                    W("dve", s_act)
                    E("dve", "tensor_mul",
                      mrg[64 * h01:64 * h01 + 64, hp * 512:hp * 512 + 512],
                      src65[0:64, :], s_bcast[:, :], inc=s_dve)

            base_o = em.val(s_wkb)
            for oc in (0, 1):
                W("pool", s_pe)
                E("pool", "dma_start", inc=s_wkb,
                  out=wkbuf[:, (oc % 2) * 1024:(oc % 2) * 1024 + 1024]
                  .rearrange("p (dc o) -> p dc o", dc=DC), in_=wo4[:, oc])
            W("pe", s_x0, em.val(s_x0))
            W("pe", s_dve)
            oevac_d, oadd_d, omm_d = {}, {}, {}

            def o_mm(oc):
                pp = ps_B[:, :] if oc % 2 == 0 else ps_A[:, 0:512]
                W("pe", s_wkb, base_o + 16 * (oc + 1))
                if oc >= 2:
                    W("pe", s_act, oevac_d[oc - 2])
                else:
                    W("pe", s_act)
                for hp in range(HP):
                    E("pe", "matmul", pp,
                      wkbuf[:, (oc % 2) * 1024 + hp * 128:(oc % 2) * 1024 + hp * 128 + 128],
                      mrg[:, hp * 512:hp * 512 + 512],
                      start=(hp == 0), stop=(hp == HP - 1),
                      inc=s_pe if hp == HP - 1 else None)
                omm_d[oc] = em.val(s_pe)
                if oc + 2 < DC:
                    W("pool", s_pe, omm_d[oc])
                    E("pool", "dma_start", inc=s_wkb,
                      out=wkbuf[:, (oc % 2) * 1024:(oc % 2) * 1024 + 1024]
                      .rearrange("p (dc o) -> p dc o", dc=DC),
                      in_=wo4[:, oc + 2])

            def o_tail(oc):
                pp = ps_B[:, :] if oc % 2 == 0 else ps_A[:, 0:512]
                kb = kraw if oc % 2 == 0 else xsq
                W("act", s_pe, omm_d[oc])
                if oc >= 2:
                    W("act", s_dve, oadd_d[oc - 2])
                else:
                    W("act", s_dve)
                E("act", "activation", inc=s_act, out=kb[:, :], in_=pp,
                  func=AF.Identity, scale=m_ga[:, oc:oc + 1])
                oevac_d[oc] = em.val(s_act)
                W("dve", s_act)
                W("dve", s_sp)
                E("dve", "tensor_add", x1_sb[:, oc * 512:oc * 512 + 512],
                  x0q[:, oc * 512:oc * 512 + 512], kb[:, :], inc=s_dve)
                oadd_d[oc] = em.val(s_dve)

            for oc in range(DC):
                o_mm(oc)
                if oc >= 1:
                    o_tail(oc - 1)
            o_tail(DC - 1)
            W("sp", s_dve)
            E("sp", "dma_start", inc=s_sp,
              out=x1d.ap().rearrange("(dc p) t -> p dc t", p=128)
              [:, :, 512 * qb:512 * qb + 512],
              in_=x1_sb[:, :].rearrange("p (dc t) -> p dc t", dc=DC))

            ln_block(x1_sb, 0, h2_sb, m_rowE2)
            for tt in range(4):
                W("dve", s_sp)
                for dc in range(DC):
                    W("pe", s_dve)
                    W("pe", s_act)
                    E("pe", "transpose", ps_tp[:, :],
                      h2_sb[:, dc * 512 + tt * 128:dc * 512 + tt * 128 + 128],
                      c_id_bf[:, :], inc=s_pe)
                    W("dve", s_pe)
                    E("dve", "tensor_copy", h2t_v[:, dc * 128:dc * 128 + 128],
                      ps_tp[:, :], inc=s_dve)
                W("sp", s_dve)
                E("sp", "dma_start", inc=s_sp,
                  out=h2d.ap()[512 * qb + 128 * tt:512 * qb + 128 * tt + 128, :],
                  in_=h2t_v[:, 0:1024])
                W("pe", s_cldg, 16)
                W("pe", s_dve)
                W("pe", s_act)
                for dc in range(DC):
                    E("pe", "matmul", B_lg,
                      h2_sb[:, dc * 512 + tt * 128:dc * 512 + tt * 128 + 128],
                      w_g[:, dc * 16:dc * 16 + 16],
                      start=(dc == 0), stop=(dc == DC - 1),
                      inc=s_pe if dc == DC - 1 else None)
                W("dve", s_pe)
                E("dve", "reduce_max", lgmax[:, :], B_lg, AX.X, negate=True,
                  inc=s_dve)
                W("act", s_dve)
                E("act", "activation", inc=s_act, out=expg[:, :], in_=B_lg,
                  func=AF.Exp, bias=lgmax[:, 0:1], accum_out=sumg[:, 0:1])
                W("dve", s_act)
                E("dve", "reciprocal", valg[:, :], sumg[:, :], inc=s_dve)
                E("dve", "reduce_max", lgmax[:, :], expg[:, :], AX.X, inc=s_dve)
                W("dve", s_dve)
                E("dve", "tensor_scalar", vig[:, :], expg[:, :], lgmax[:, 0:1],
                  None, ALU.is_ge, inc=s_dve)
                W("dve", s_dve)
                E("dve", "tensor_mul", vig[:, :], vig[:, :], c_iotam[:, :],
                  inc=s_dve)
                W("dve", s_dve)
                E("dve", "tensor_reduce", eg[:, :], vig[:, :], AX.X, ALU.min,
                  inc=s_dve)
                W("dve", s_dve)
                E("dve", "tensor_scalar", rec2[:, 1:2], eg[:, :], BIG, None,
                  ALU.add, inc=s_dve)
                E("dve", "tensor_copy", rec2[:, 0:1], valg[:, :], inc=s_dve)
                W("sp", s_dve)
                E("sp", "dma_start", inc=s_sp,
                  out=rt_in.ap()[512 * qb + 128 * tt:512 * qb + 128 * tt + 128, :],
                  in_=rec2[:, :])
            W("pool", s_dve)
            em.inc("pool", s_xfree)
            W("pool", s_sp, em.val(s_sp))
            E("pool", "collective_compute", "AllGather", ALU.bypass,
              replica_groups=[list(range(NCORE))],
              ins=[rt_in.ap()[512 * qb:512 * qb + 512, :]],
              outs=[rt_out.ap()[4096 * qb:4096 * qb + 4096, :]],
              inc=s_cc, amt=1)
            E("pool", "collective_compute", "AllGather", ALU.bypass,
              replica_groups=[list(range(NCORE))],
              ins=[h2d.ap()[512 * qb:512 * qb + 512, :]],
              outs=[h2ag.ap()[4096 * qb:4096 * qb + 4096, :]],
              inc=s_cc, amt=1)

        tap("x1d", x1d.ap())
        tap("h2d", h2d.ap())
        tap("rt_in", rt_in.ap())

        em.mute = LV < 3
        W("sp", s_pe)
        W("sp", s_dve)
        for dst, src in [
            (c_us, din["ustrict_bf"]), (c_agpos, din["agpos_f32"]),
            (c_base, din["baseB_f32"]), (c_lbase, din["localbase"]),
            (c_perm, din["permT"]), (c_i128, din["iota128_f32"]),
            (c_i20, din["iota20_f32"]), (c_iotap, din["iotap_f32"]),
        ]:
            E("sp", "dma_start", inc=s_p3c, out=dst[:, :], in_=src)
        W("pool", s_cc, 8)
        E("pool", "dma_start", inc=s_gx, out=g_pk[:, :],
          in_=rt_out.ap().rearrange("(p f) c -> p (f c)", p=128))
        gx_rt = em.val(s_gx)
        em.mute = em.mute or KS < 3
        E("pool", "dma_start", inc=s_wf,
          out=w1_sb[:, 0:DC * 2048].rearrange("p (dc h) -> p dc h", dc=DC),
          in_=din["w1a"].rearrange("(dc p) h -> p dc h", p=128))
        E("pool", "dma_start", inc=s_wf,
          out=w2_sb[:, :].rearrange("p (hc o) -> p hc o", hc=16),
          in_=din["w2a"].rearrange("(hc p) o -> p hc o", p=128))
        E("pool", "dma_start", inc=s_wf,
          out=w1_sb[:, DC * 2048:2 * DC * 2048]
          .rearrange("p (dc h) -> p dc h", dc=DC),
          in_=din["w1b"].rearrange("(dc p) h -> p dc h", p=128))
        E("pool", "dma_start", inc=s_cldg, out=b2rowa[:, :], in_=din["b2pa"])
        E("pool", "dma_start", inc=s_cldg, out=b2rowb[:, :], in_=din["b2pb"])

        em.mute = em.mute or KS < 4
        W("dve", s_gx, gx_rt)
        W("dve", s_p3c, em.val(s_p3c))
        pk3 = g_pk[:, :].rearrange("p (f c) -> p f c", c=2)
        E("dve", "tensor_copy", g_val[:, :].rearrange("p (f c) -> p f c", c=1),
          pk3[:, :, 0:1], inc=s_dve)
        E("dve", "tensor_copy", g_e[:, :].rearrange("p (f c) -> p f c", c=1),
          pk3[:, :, 1:2], inc=s_dve)
        W("dve", s_dve)
        for n_ in range(16):
            E("dve", "tensor_scalar",
              g_oh[:, :].rearrange("p (f i) -> p f i", i=16)[:, :, n_:n_ + 1],
              g_e[:, :].rearrange("p (f one) -> p f one", one=1),
              float(n_), None, ALU.is_equal, inc=s_dve)
        W("dve", s_dve)
        E("dve", "tensor_copy", g_pfA[:, :], g_oh[:, :], inc=s_dve)
        shift, cur, nxt = 16, g_pfA, g_pfB
        while shift < 2048:
            W("dve", s_dve)
            E("dve", "tensor_copy", nxt[:, 0:shift], cur[:, 0:shift], inc=s_dve)
            E("dve", "tensor_add", nxt[:, shift:2048], cur[:, shift:2048],
              cur[:, 0:2048 - shift], inc=s_dve)
            cur, nxt = nxt, cur
            shift *= 2
        g_incl = cur
        g_oth = g_pfA if g_incl is g_pfB else g_pfB
        W("dve", s_dve)
        E("dve", "tensor_copy", g_T[:, :], g_incl[:, 127 * 16:127 * 16 + 16],
          inc=s_dve)
        W("pe", s_dve)
        W("pe", s_act)
        E("pe", "matmul", A_offs, c_us[:, :], g_T[:, :], start=True, stop=True,
          inc=s_pe)
        W("dve", s_pe)
        E("dve", "tensor_sub", g_rank[:, :], g_incl[:, :], g_oh[:, :],
          inc=s_dve)
        W("dve", s_dve)
        E("dve", "tensor_add", g_rank[:, :].rearrange("p (f i) -> p f i", i=16),
          g_rank[:, :].rearrange("p (f i) -> p f i", i=16),
          A_offs.rearrange("p (one i) -> p one i", one=1)
          .broadcast_to((128, 128, 16)), inc=s_dve)
        W("dve", s_dve)
        E("dve", "tensor_mul", g_oth[:, :], g_rank[:, :], g_oh[:, :],
          inc=s_dve)
        W("dve", s_dve)
        E("dve", "tensor_reduce", g_rt[:, :],
          g_oth[:, :].rearrange("p (f i) -> p f i", i=16), AX.X, ALU.add,
          inc=s_dve)
        W("dve", s_dve)
        E("dve", "tensor_mul", g_oth[:, :].rearrange("p (f i) -> p f i", i=16),
          g_oh[:, :].rearrange("p (f i) -> p f i", i=16),
          c_base[:, :].rearrange("p (one i) -> p one i", one=1)
          .broadcast_to((128, 128, 16)), inc=s_dve)
        W("dve", s_dve)
        E("dve", "tensor_reduce", g_fetch[:, :],
          g_oth[:, :].rearrange("p (f i) -> p f i", i=16), AX.X, ALU.add,
          inc=s_dve)
        W("dve", s_dve)
        E("dve", "tensor_add", g_fetch[:, :], g_fetch[:, :], g_rt[:, :],
          inc=s_dve)
        E("dve", "tensor_scalar", g_sel[:, :], g_rt[:, :], float(CAP), None,
          ALU.is_ge, inc=s_dve)
        W("dve", s_dve)
        E("dve", "tensor_scalar", g_tmp[:, :], g_sel[:, :], -1.0, 1.0,
          ALU.mult, ALU.add, inc=s_dve)
        W("dve", s_dve)
        E("dve", "tensor_mul", g_fetch[:, :], g_fetch[:, :], g_tmp[:, :],
          inc=s_dve)
        E("dve", "tensor_mul", g_oth[:, :].rearrange("p (f i) -> p f i", i=16),
          g_oh[:, :].rearrange("p (f i) -> p f i", i=16),
          c_lbase[:, :].rearrange("p (one i) -> p one i", one=1)
          .broadcast_to((128, 128, 16)), inc=s_dve)
        W("dve", s_dve)
        E("dve", "tensor_reduce", g_scat[:, :],
          g_oth[:, :].rearrange("p (f i) -> p f i", i=16), AX.X, ALU.add,
          inc=s_dve)
        W("dve", s_dve)
        E("dve", "tensor_add", g_scat[:, :], g_scat[:, :], g_rt[:, :],
          inc=s_dve)
        E("dve", "tensor_scalar", g_tmp[:, :], g_sel[:, :], BIG, None,
          ALU.mult, inc=s_dve)
        W("dve", s_dve)
        E("dve", "tensor_add", g_scat[:, :], g_scat[:, :], g_tmp[:, :],
          inc=s_dve)
        E("dve", "tensor_tensor", g_lhs[:, :], c_i128[:, :],
          c_iotap[:, 0:1].broadcast_to((128, 128)), ALU.is_equal, inc=s_dve)
        for srcg, dstf in ((g_fetch, None), (g_sel, g_selT), (g_scat, g_sT)):
            W("pe", s_dve)
            W("pe", s_act)
            ident = c_perm if dstf is not g_sT else g_lhs
            E("pe", "transpose", A_tpf[:, 0:128], srcg[:, :], ident[:, :],
              inc=s_pe)
            W("dve", s_pe)
            if dstf is None:
                E("dve", "tensor_copy", g_tmp[:, :], A_tpf[:, 0:128],
                  inc=s_dve)
                W("dve", s_dve)
                E("dve", "tensor_copy", g_feU[:, :], g_tmp[:, :], inc=s_dve)
            else:
                E("dve", "tensor_copy", dstf[:, :], A_tpf[:, 0:128],
                  inc=s_dve)
        em.mute = em.mute or KS < 5
        W("dve", s_dve)
        E("dve", "tensor_scalar", g_e[:, :], g_sT[:, :], 128.0, None,
          ALU.is_ge, inc=s_dve)
        for u in range(2, 21):
            E("dve", "tensor_scalar", g_tmp[:, :], g_sT[:, :],
              float(128 * u), None, ALU.is_ge, inc=s_dve)
            E("dve", "tensor_add", g_e[:, :], g_e[:, :], g_tmp[:, :],
              inc=s_dve)
        E("dve", "tensor_scalar", g_val[:, :], g_e[:, :], -128.0, None,
          ALU.mult, inc=s_dve)
        E("dve", "tensor_add", g_val[:, :], g_val[:, :], g_sT[:, :],
          inc=s_dve)
        W("dve", s_pe)
        pe0 = em.val(s_pe)
        for k in range(128):
            lhs = g_lhs if k % 2 == 0 else g_lhs2
            rhs = g_rhs if k % 2 == 0 else g_rhs2
            if k >= 2:
                W("dve", s_pe, pe0 + k - 1)
            E("dve", "tensor_tensor", lhs[:, :], c_i128[:, :],
              g_val[:, k:k + 1].broadcast_to((128, 128)), ALU.is_equal,
              inc=s_dve)
            E("dve", "tensor_scalar", g_tsc[:, :], c_iotap[:, :],
              float(128 * k + 1), None, ALU.add, inc=s_dve)
            E("dve", "tensor_tensor", rhs[:, :], c_i20[:, :],
              g_e[:, k:k + 1].broadcast_to((128, 20)), ALU.is_equal,
              inc=s_dve)
            E("dve", "tensor_scalar", lhs[:, :], lhs[:, :], g_tsc[:, 0:1],
              None, ALU.mult, inc=s_dve)
            W("pe", s_dve)
            E("pe", "matmul", ps_B[:, 0:20], lhs[:, :], rhs[:, :],
              start=(k == 0), stop=(k == 127), inc=s_pe)
        E("dve", "memset", g_tsc[:, :], -1.0, inc=s_dve)
        W("act", s_pe)
        W("act", s_dve)
        E("act", "activation", inc=s_act, out=g_sT[:, 0:20],
          in_=ps_B[:, 0:20], func=AF.Relu, bias=g_tsc[:, 0:1])
        W("dve", s_act)
        E("dve", "tensor_copy", g_gU[:, :], g_sT[:, 0:20], inc=s_dve)
        dve_rank = em.val(s_dve)

        tap("g_rt", g_rt[:, :])
        tap("g_fetch", g_fetch[:, :])
        tap("g_gU", g_sT[:, 0:20])

        em.mute = LV < 4
        W("pool", s_cc, 8)
        W("pool", s_dve, dve_rank)
        for e2 in range(2):
            if e2 == 1:
                W("pool", s_pe)
                E("pool", "dma_start", inc=s_wf,
                  out=w2_sb[:, :].rearrange("p (hc o) -> p hc o", hc=16),
                  in_=din["w2b"].rearrange("(hc p) o -> p hc o", p=128))
            for kcall in range(10):
                col = e2 * 10 + kcall
                E("pool", "indirect_dma_start", inc=s_gx,
                  out=xe_sb[:, 1024 * kcall:1024 * kcall + 1024],
                  out_offset=None, in_=h2ag.ap(),
                  in_offset=IndirectOffsetOnAxis(ap=g_gU[:, col:col + 1],
                                                 axis=0),
                  bounds_check=BT - 1, oob_is_err=False)
                W("pool", s_gx)
            gx_done = em.val(s_gx)
            b2r = b2rowa if e2 == 0 else b2rowb
            W("pe", s_cldg)
            W("pe", s_act)
            W("pe", s_dve)
            for dcb in range(2):
                W("pe", s_act)
                E("pe", "matmul", A_bc, c_ones_r[0:1, 0:64],
                  b2r[0:1, dcb * 512:dcb * 512 + 512], start=True, stop=True,
                  inc=s_pe)
                W("act", s_pe)
                E("act", "activation", inc=s_act,
                  out=b2B[0:64, dcb * 512:dcb * 512 + 512], in_=A_bc,
                  func=AF.Identity)
            W("sp", s_act)
            E("sp", "dma_start", inc=s_modb, out=b2B[64:128, :],
              in_=b2B[0:64, :])
            spb2 = em.val(s_modb)

            for (t0, t1) in ((0, 4), (4, 8), (8, 10)):
                ng = (t1 - t0) * 128
                W("pe", s_gx, gx_done)
                W("pe", s_wf, 16 * (3 if e2 == 0 else 4))
                for dc in range(DC):
                    for tt in range(t0, t1):
                        W("pe", s_dve)
                        E("pe", "transpose", ps_tp[:, :],
                          xe_sb[:, tt * 1024 + dc * 128:tt * 1024 + dc * 128 + 128],
                          c_id_bf[:, :], inc=s_pe)
                        W("dve", s_pe)
                        E("dve", "tensor_copy",
                          xeT[:, dc * 512 + (tt - t0) * 128:
                              dc * 512 + (tt - t0) * 128 + 128],
                          ps_tp[:, :], inc=s_dve)
                W("pe", s_dve)
                geld = {}
                for hc in range(16):
                    off = 512 * (hc % 2)
                    if hc >= 2:
                        W("pe", s_act, geld[hc - 2])
                    else:
                        W("pe", s_dve)
                    for dc in range(DC):
                        E("pe", "matmul", ps_A[:, off:off + ng],
                          w1_sb[:, e2 * DC * 2048 + dc * 2048 + hc * 128:
                                e2 * DC * 2048 + dc * 2048 + hc * 128 + 128],
                          xeT[:, dc * 512:dc * 512 + ng],
                          start=(dc == 0), stop=(dc == DC - 1),
                          inc=s_pe if dc == DC - 1 else None)
                    W("act", s_pe, em.val(s_pe))
                    E("act", "activation", inc=s_act,
                      out=hTs[:, hc * 512:hc * 512 + ng],
                      in_=ps_A[:, off:off + ng],
                      func=AF.Gelu_apprx_tanh,
                      bias=b1_sb[:, e2 * 16 + hc:e2 * 16 + hc + 1])
                    geld[hc] = em.val(s_act)
                W("pe", s_act)
                W("pe", s_modb, spb2)
                addv = {}
                for tt in range(t0, t1):
                    for dcb in range(2):
                        k2 = (tt - t0) * 2 + dcb
                        pp2 = ps_B[:, :] if k2 % 2 == 0 else ps_A[:, 0:512]
                        if k2 >= 2:
                            W("pe", s_dve, addv[k2 - 2])
                        else:
                            W("pe", s_dve)
                        for hc in range(16):
                            E("pe", "matmul", pp2,
                              hTs[:, hc * 512 + (tt - t0) * 128:
                                  hc * 512 + (tt - t0) * 128 + 128],
                              w2_sb[:, hc * 1024 + dcb * 512:
                                    hc * 1024 + dcb * 512 + 512],
                              start=(hc == 0), stop=(hc == 15),
                              inc=s_pe if hc == 15 else None)
                        W("dve", s_pe, em.val(s_pe))
                        W("dve", s_sp)
                        E("dve", "tensor_add",
                          yo_sb[:, dcb * 512:dcb * 512 + 512], pp2,
                          b2B[:, dcb * 512:dcb * 512 + 512], inc=s_dve)
                        addv[k2] = em.val(s_dve)
                    W("sp", s_dve)
                    E("sp", "dma_start", inc=s_sp,
                      out=outblk.ap()[e2 * CAP + tt * 128:e2 * CAP + tt * 128 + 128, :],
                      in_=yo_sb[:, :])
            W("pool", s_sp, em.val(s_sp))
            E("pool", "collective_compute", "AllGather", ALU.bypass,
              replica_groups=[list(range(NCORE))],
              ins=[outblk.ap()[e2 * CAP:e2 * CAP + CAP, :]],
              outs=[outag.ap()[e2 * NCORE * CAP:(e2 + 1) * NCORE * CAP, :]],
              inc=s_cc, amt=1)

        tap("outblk", outblk.ap())

        em.mute = LV < 5
        x1d3 = x1d.ap().rearrange("(dc p) t -> p dc t", p=128)
        yout3 = yout.rearrange("(dc p) t -> p dc t", p=128)
        W("pool", s_cc, 10)
        f_add, f_tp, f_sel, f_fin, f_store = {}, {}, {}, {}, {}
        for m in range(16):
            zS = z_sb if m % 2 == 0 else z_sbB
            hS = h2row if m % 2 == 0 else h2rowB
            xS = x1r if m % 2 == 0 else x1rB
            dS = zdiff if m % 2 == 0 else zdiffB
            sS = zsel if m % 2 == 0 else zselB
            if m >= 2:
                W("pool", s_dve, f_add[m - 2])
                W("pool", s_pe, f_tp[m - 2])
            else:
                W("pool", s_pe)
                W("pool", s_dve)
            E("pool", "indirect_dma_start", inc=s_fin, out=zS[:, :],
              out_offset=None, in_=outag.ap(),
              in_offset=IndirectOffsetOnAxis(ap=g_feU[:, m:m + 1], axis=0),
              bounds_check=NCORE * 2 * CAP - 1, oob_is_err=False)
            E("pool", "dma_start", inc=s_fin, out=hS[:, :],
              in_=h2d.ap()[128 * m:128 * m + 128, :])
            E("pool", "dma_start", inc=s_fin, out=xS[:, :]
              .rearrange("p (dc t) -> p dc t", dc=DC),
              in_=x1d3[:, :, 128 * m:128 * m + 128])
            f_fin[m] = em.val(s_fin)
            W("pool", s_fin, f_fin[m])
            W("dve", s_fin, f_fin[m])
            E("dve", "tensor_sub", dS[:, :], hS[:, :], zS[:, :], inc=s_dve)
            E("dve", "tensor_scalar", sS[:, :], dS[:, :],
              g_selT[:, m:m + 1], None, ALU.mult, inc=s_dve)
            E("dve", "tensor_add", dS[:, :], sS[:, :], zS[:, :], inc=s_dve)
            f_sel[m] = em.val(s_dve)
            tpd, actd, addd = {}, {}, {}
            for dc in range(DC):
                tp = ps_A[:, 0:128] if dc % 2 == 0 else ps_A[:, 512:640]
                zg = zgb[:, :] if dc % 2 == 0 else kt1[:, 0:128]
                if dc >= 2:
                    W("pe", s_act, actd[dc - 2])
                elif dc == 0:
                    W("pe", s_dve, f_sel[m])
                    W("pe", s_act)
                E("pe", "matmul", tp,
                  dS[:, dc * 128:dc * 128 + 128], c_id_bf[:, :],
                  start=True, stop=True, inc=s_pe)
                tpd[dc] = em.val(s_pe)
                W("act", s_pe, tpd[dc])
                if dc >= 2:
                    W("act", s_dve, addd[dc - 2])
                E("act", "activation", inc=s_act, out=zg,
                  in_=tp, func=AF.Identity, scale=m_gb[:, dc:dc + 1])
                actd[dc] = em.val(s_act)
                W("dve", s_act, actd[dc])
                if dc == 0:
                    W("dve", s_out)
                E("dve", "tensor_add", yf_sb[:, dc * 128:dc * 128 + 128],
                  xS[:, dc * 128:dc * 128 + 128], zg, inc=s_dve)
                addd[dc] = em.val(s_dve)
            f_add[m] = em.val(s_dve)
            f_tp[m] = em.val(s_pe)
            W("sp", s_dve, f_add[m])
            E("sp", "dma_start", inc=s_out,
              out=yout3[:, :, 128 * m:128 * m + 128],
              in_=yf_sb[:, :].rearrange("p (dc t) -> p dc t", dc=DC))
        W("pool", s_out)
        if tap_out:
            W("pool", s_tap)

        block = es.enter_context(nc.Block())
        em.realize(block)

    return nc


_CACHE = {}


def kernel(**inputs):
    consts = _host_constants()
    if "nc" not in _CACHE:
        _CACHE["nc"] = build_program()
    nc = _CACHE["nc"]
    in_maps = []
    for c in range(NCORE):
        d = _per_core_inputs(inputs, c)
        d.update(consts)
        in_maps.append(d)
    res = run_bass_kernel_spmd(nc, in_maps, core_ids=list(range(NCORE)))
    out = np.empty((B, S, D), dtype=np.float32)
    for c in range(NCORE):
        y = np.asarray(res.results[c]["yout"])
        b, half = c // 2, c % 2
        for j in range(QB):
            g = 2 * j + half
            out[b, 512 * g:512 * g + 512, :] = y[:, 512 * j:512 * j + 512].T
    return out
